# version 09 (final)
# speedup vs baseline: 1.3781x; 1.0138x over previous
import contextlib
import math
import numpy as np

import concourse.bass as bass
import concourse.mybir as mybir
from concourse.bass_utils import run_bass_kernel_spmd
from concourse.alu_op_type import AluOpType

F32 = mybir.dt.float32
BF16 = mybir.dt.bfloat16
I32 = mybir.dt.int32
AF = mybir.ActivationFunctionType
AX = mybir.AxisListType
OP = AluOpType

NCORES = 8
B, S, D = 2, 8192, 1024
TPC = 2048
NT = TPC // 128
ND = D // 128
CSHIFT = 32.0

ALL_BLOCKS = ["L0_attn", "L0_ffn", "L1_attn", "L1_ffn",
              "L2_attn", "L2_ffn", "L3_attn", "L3_ffn"]


class Sem:
    def __init__(self, handle):
        self.h = handle
        self.n = 0

    def inc(self, inst, amount=1):
        self.n += amount
        inst.then_inc(self.h, amount)
        return self.n


class Prog:
    def __init__(self, nc, core, stack):
        self.nc = nc
        self.core = core
        self.group = core // 4
        self.rank = core % 4
        self.stack = stack
        self.scopes = [stack]
        self.sems = {}
        self.sb = {}
        self.dram_handles = {}
        self._uid = 0
        self._dynregs = {}
        self._sem_pool = []
        self._drain_list = []

    def sync_reg(self, key, fn):
        if key not in self._dynregs:
            self._dynregs[key] = fn()
        return self._dynregs[key]

    def pid4(self):
        return self.sync_reg("pid4", lambda: self.nc.partition_id() % 4)

    def uid(self):
        self._uid += 1
        return self._uid

    def sem(self, name):
        if name not in self.sems:
            es = self.scopes[-1]
            if es is self.stack:
                h = self.stack.enter_context(
                    self.nc.semaphore(f"{name}_{self.uid()}"))
                self.sems[name] = Sem(h)
            else:
                if self._sem_pool:
                    h, base = self._sem_pool.pop()
                    s = Sem(h)
                    s.n = base
                else:
                    h = self.stack.enter_context(
                        self.nc.semaphore(f"{name}_{self.uid()}"))
                    s = Sem(h)
                self.sems[name] = s
                es.callback(self._pool_sem, name)
        return self.sems[name]

    def _pool_sem(self, name):
        s = self.sems.pop(name, None)
        if s is not None:
            self._drain_list.append((s.h, s.n))

    @contextlib.contextmanager
    def scope(self):
        es = contextlib.ExitStack()
        self.scopes.append(es)
        try:
            with es:
                yield
        finally:
            self.scopes.pop()
            drains = self._drain_list
            if drains:
                self._drain_list = []
                with self.nc.Block() as blk:
                    @blk.sync
                    def _(sync):
                        for h, n in drains:
                            sync.wait_ge(h, n)
                self._sem_pool.extend(drains)

    def sbuf(self, name, shape, dtype, persist=False):
        tgt = self.stack if persist else self.scopes[-1]
        h = tgt.enter_context(self.nc.sbuf_tensor(name + f"_{self.uid()}",
                                                  shape, dtype))
        if persist:
            self.sb[name] = h
        return h

    def psum(self, name, shape, dtype=F32):
        return self.scopes[-1].enter_context(
            self.nc.psum_tensor(name + f"_{self.uid()}", shape, dtype))

    def dram(self, name, shape, dtype, addr_space=None, kind=None):
        if name in self.dram_handles:
            return self.dram_handles[name]
        kw = {}
        if addr_space is not None:
            kw["addr_space"] = addr_space
        if kind is not None:
            kw["kind"] = kind
        h = self.nc.dram_tensor(name, list(shape), dtype, **kw)
        self.dram_handles[name] = h
        return h

    def block(self):
        return self.nc.Block()


def np_bf16(x):
    import ml_dtypes
    return np.ascontiguousarray(x).astype(ml_dtypes.bfloat16)



def emit_const_setup(p):
    ident = p.sbuf("ident", [128, 128], BF16, persist=True)
    iota_p = p.sbuf("iota_p", [128, 128], F32, persist=True)
    iota_f = p.sbuf("iota_f", [128, 128], F32, persist=True)
    ones_col = p.sbuf("ones_col", [128, 1], BF16, persist=True)
    ones_one = p.sbuf("ones_one", [1, 128], BF16, persist=True)
    with p.block() as blk:
        @blk.gpsimd
        def _(gpsimd):
            gpsimd.iota(iota_f.ap(), [[1, 128]], channel_multiplier=0,
                        allow_small_or_imprecise_dtypes=True)
            gpsimd.iota(iota_p.ap(), [[0, 128]], channel_multiplier=1,
                        allow_small_or_imprecise_dtypes=True)
            gpsimd.memset(ones_col.ap(), 1.0)
            gpsimd.memset(ones_one.ap(), 1.0)
    with p.block() as blk:
        @blk.vector
        def _(vector):
            vector.tensor_tensor(ident.ap(), iota_f.ap(), iota_p.ap(),
                                 OP.is_equal)


def emit_pre_ln(p, xhat, eps=1e-5, src=None, xhat2=None, gain_bias=None):
    nc = p.nc
    src = src if src is not None else p.sb["xres"]
    with p.scope():
        stats = p.sbuf("ln_stats", [128, 12 * NT], F32)
        mv = p.sbuf("ln_mv", [128, 2 * NT], F32)
        veps = p.sbuf("ln_veps", [128, NT], F32)
        rstd = p.sbuf("ln_rstd", [128, NT], F32)
        lnv = p.sbuf("ln_lnv", [128, NT], F32)
        nb = p.sbuf("ln_nb", [128, NT], F32)
        sv = p.sem("lnv_s")
        ss = p.sem("lns_s")
        with p.block() as blk:
            @blk.vector
            def _(vector):
                for t in range(NT):
                    xt = src.ap()[:, t * 1024:(t + 1) * 1024]
                    vector.bn_stats(stats.ap()[:, 12 * t:12 * t + 6], xt[:, 0:512])
                    inst = vector.bn_stats(stats.ap()[:, 12 * t + 6:12 * t + 12],
                                           xt[:, 512:1024])
                inst.then_inc(sv.h, 1)
                sv.n += 1
                vector.wait_ge(sv.h, sv.n)
                for t in range(NT):
                    vector.bn_aggr(mv.ap()[:, 2 * t:2 * t + 2],
                                   stats.ap()[:, 12 * t:12 * t + 12])
                for t in range(NT):
                    inst = vector.tensor_scalar(veps.ap()[:, t:t + 1],
                                                mv.ap()[:, 2 * t + 1:2 * t + 2],
                                                eps, None, OP.add)
                inst.then_inc(sv.h, 1)
                sv.n += 1

            @blk.scalar
            def _(scalar):
                scalar.wait_ge(sv.h, sv.n)
                scalar.activation(lnv.ap(), veps.ap(), AF.Ln).then_inc(ss.h, 1)
                ss.n += 1
                scalar.wait_ge(ss.h, ss.n)
                scalar.activation(rstd.ap(), lnv.ap(), AF.Exp,
                                  scale=-0.5).then_inc(ss.h, 1)
                ss.n += 1

            @blk.vector
            def _(vector):
                vector.wait_ge(ss.h, ss.n)
                for t in range(NT):
                    inst = vector.scalar_tensor_tensor(
                        nb.ap()[:, t:t + 1], mv.ap()[:, 2 * t:2 * t + 1], -1.0,
                        rstd.ap()[:, t:t + 1], OP.mult, OP.mult)
                inst.then_inc(sv.h, 1)
                sv.n += 1
                vector.wait_ge(sv.h, sv.n)
                for t in range(NT):
                    xt = src.ap()[:, t * 1024:(t + 1) * 1024]
                    dst = xhat.ap()[:, t * 1024:(t + 1) * 1024]
                    inst = vector.tensor_scalar(dst, xt, rstd.ap()[:, t:t + 1],
                                                nb.ap()[:, t:t + 1], OP.mult, OP.add)
                    if xhat2 is not None:
                        inst = vector.tensor_scalar(
                            xhat2.ap()[:, t * 1024:(t + 1) * 1024], xt,
                            rstd.ap()[:, t:t + 1], nb.ap()[:, t:t + 1],
                            OP.mult, OP.add)
                if gain_bias is not None:
                    gbc, bbc = gain_bias
                    inst.then_inc(sv.h, 1)
                    sv.n += 1
                    vector.wait_ge(sv.h, sv.n)
                    for t in range(NT):
                        dst = xhat.ap()[:, t * 1024:(t + 1) * 1024]
                        inst = vector.tensor_tensor(dst, dst, gbc.ap(), OP.mult)
                    inst.then_inc(sv.h, 1)
                    sv.n += 1
                    vector.wait_ge(sv.h, sv.n)
                    for t in range(NT):
                        vector.tensor_tensor(
                            xhat.ap()[:, t * 1024:(t + 1) * 1024],
                            xhat.ap()[:, t * 1024:(t + 1) * 1024],
                            bbc.ap(), OP.add)


def emit_transpose_many(p, pairs, ident=None):
    nc = p.nc
    ident = ident if ident is not None else p.sb["ident"]
    NBUF = 4
    with p.scope():
        ptr = [p.psum("trp", [128, 128], BF16) for _ in range(NBUF)]
        te = p.sem("tr_te")
        evs = p.sem("tr_evs")
        evv = p.sem("tr_evv")
        te0, evs0, evv0 = te.n, evs.n, evv.n
        n = len(pairs)
        te.n += n
        evs.n += (n + 1) // 2
        evv.n += n // 2
        with p.block() as blk:
            @blk.tensor
            def _(tensor):
                for i, (s, d) in enumerate(pairs):
                    pi, fi = s.shape[0], s.shape[1]
                    if i >= NBUF:
                        j = i - NBUF
                        if j % 2 == 0:
                            tensor.wait_ge(evs.h, evs0 + j // 2 + 1)
                        else:
                            tensor.wait_ge(evv.h, evv0 + (j + 1) // 2)
                    tensor.transpose(ptr[i % NBUF].ap()[0:fi, 0:pi], s,
                                     ident.ap()[0:pi, 0:pi]).then_inc(te.h, 1)

            @blk.scalar
            def _(scalar):
                for i, (s, d) in enumerate(pairs):
                    pi, fi = s.shape[0], s.shape[1]
                    if i % 2 == 0:
                        scalar.wait_ge(te.h, te0 + i + 1)
                        scalar.copy(d, ptr[i % NBUF].ap()[0:fi, 0:pi]
                                    ).then_inc(evs.h, 1)

            @blk.vector
            def _(vector):
                for i, (s, d) in enumerate(pairs):
                    pi, fi = s.shape[0], s.shape[1]
                    if i % 2 == 1:
                        vector.wait_ge(te.h, te0 + i + 1)
                        vector.tensor_copy(d, ptr[i % NBUF].ap()[0:fi, 0:pi]
                                           ).then_inc(evv.h, 1)


def emit_tm_to_fm(p, src, dst, ncols=NT, dst_off=0):
    T = dst.ap().shape[1] // ND
    pairs = []
    for t in range(ncols):
        for d in range(ND):
            pairs.append((
                src.ap()[:, t * 1024 + d * 128:t * 1024 + (d + 1) * 128],
                dst.ap()[:, d * T + dst_off + t * 128:
                         d * T + dst_off + (t + 1) * 128]))
    emit_transpose_many(p, pairs)


def emit_allgather(p, name, sb_src_ap, shape2d, dtype, groups="batch"):
    nc = p.nc
    if groups == "batch":
        rg = [[0, 1, 2, 3], [4, 5, 6, 7]]
        nr = 4
        space = "Local"
    else:
        rg = [list(range(8))]
        nr = 8
        space = "Shared"
    cin = p.dram(f"cc_{name}_in", shape2d, dtype)
    cout = p.dram(f"cc_{name}_out", [nr * shape2d[0], shape2d[1]], dtype,
                  addr_space=space)
    st = p.sem("cc_st")
    cc = p.sem("cc_done")
    if len(sb_src_ap.shape) == 3:
        pdim = sb_src_ap.shape[0]
        cin_ap = cin.ap().rearrange("(c p) t -> p c t", p=pdim)
    else:
        cin_ap = cin.ap()
    with p.block() as blk:
        @blk.sync
        def _(sync):
            sync.dma_start(out=cin_ap, in_=sb_src_ap).then_inc(st.h, 16)
            st.n += 16
    with p.block() as blk:
        tok = st.n

        @blk.gpsimd
        def _(gpsimd):
            gpsimd.wait_ge(st.h, tok)
            cc.inc(gpsimd.collective_compute(
                "AllGather", OP.bypass, replica_groups=rg,
                ins=[cin.ap().opt()], outs=[cout.ap().opt()]))
    with p.block() as blk:
        tok = cc.n

        @blk.vector
        def _(vector):
            vector.wait_ge(cc.h, tok)
    return cout



def prep_L3_ffn(core, inputs):
    return {
        "l3f_wgT": np.ascontiguousarray(inputs["L3_ffn_w_gate"].T),
        "l3f_wuT": np.ascontiguousarray(inputs["L3_ffn_w_up"].T),
        "l3f_wdT": np.ascontiguousarray(inputs["L3_ffn_w_down"].T),
    }


def emit_L3_ffn(p):
    nc = p.nc
    H = 3072
    NH = H // 128
    wgT = p.dram("l3f_wgT", [D, H], I32, kind="ExternalInput")
    wuT = p.dram("l3f_wuT", [D, H], I32, kind="ExternalInput")
    wdT = p.dram("l3f_wdT", [H, D], I32, kind="ExternalInput")
    xres = p.sb["xres"]
    with p.scope():
        xf = p.sbuf("xhat_fm", [128, ND * TPC], BF16)
        with p.scope():
            xhat_tm = p.sbuf("xhat_tm", [128, NT * 1024], BF16)
            emit_pre_ln(p, xhat_tm)
            emit_tm_to_fm(p, xhat_tm, xf)

        ld = p.sem("ld")
        vc = p.sem("vc")
        te = p.sem("te")
        se = p.sem("se")
        vh = p.sem("vh")
        ve = p.sem("ve")

        wint = [p.sbuf("l3f_wint", [128, 2048], I32) for _ in range(2)]
        wbf = [p.sbuf("l3f_wbf", [128, 2048], BF16) for _ in range(2)]
        hbuf = p.sbuf("l3f_h", [128, NH * 512], BF16)
        gtmp = [p.sbuf("l3f_gtmp", [128, 512], F32) for _ in range(2)]

        for tt in range(TPC // 512):
            with p.scope():
                pg = [p.psum("l3f_pg", [128, 512]) for _ in range(2)]
                pu = [p.psum("l3f_pu", [128, 512]) for _ in range(2)]
                loaded = [ld.n + 32 * (i + 1) for i in range(NH)]
                casted = [vc.n + (i + 1) for i in range(NH)]
                mm_g = [te.n + 2 * i + 1 for i in range(NH)]
                mm_u = [te.n + 2 * i + 2 for i in range(NH)]
                sil_done = [se.n + (i + 1) for i in range(NH)]
                mul_done = [vh.n + (i + 1) for i in range(NH)]
                with p.block() as blk:
                    @blk.sync
                    def _(sync):
                        for hc in range(NH):
                            b = hc % 2
                            if hc >= 2:
                                sync.wait_ge(vc.h, casted[hc - 2])
                            sync.dma_start(
                                out=wint[b].ap()[:, 0:1024]
                                    .rearrange("p (c h) -> p c h", c=8),
                                in_=wgT.ap()[:, hc * 128:(hc + 1) * 128]
                                    .rearrange("(c p) h -> p c h", p=128)
                            ).then_inc(ld.h, 16)
                            sync.dma_start(
                                out=wint[b].ap()[:, 1024:2048]
                                    .rearrange("p (c h) -> p c h", c=8),
                                in_=wuT.ap()[:, hc * 128:(hc + 1) * 128]
                                    .rearrange("(c p) h -> p c h", p=128)
                            ).then_inc(ld.h, 16)

                    @blk.vector
                    def _(vector):
                        for hc in range(NH + 1):
                            b = hc % 2
                            if hc < NH:
                                vector.wait_ge(ld.h, loaded[hc])
                                if hc >= 2:
                                    vector.wait_ge(te.h, mm_u[hc - 2])
                                vector.tensor_scalar(
                                    wbf[b].ap(), wint[b].ap(), 1.0 / 127.0,
                                    None, OP.mult).then_inc(vc.h, 1)
                            if hc >= 1:
                                j = hc - 1
                                vector.wait_ge(te.h, mm_u[j])
                                vector.wait_ge(se.h, sil_done[j])
                                vector.tensor_tensor(
                                    hbuf.ap()[:, j * 512:(j + 1) * 512],
                                    gtmp[j % 2].ap(), pu[j % 2].ap(),
                                    OP.mult).then_inc(vh.h, 1)

                    @blk.tensor
                    def _(tensor):
                        for hc in range(NH):
                            b = hc % 2
                            tensor.wait_ge(vc.h, casted[hc])
                            if hc >= 2:
                                tensor.wait_ge(se.h, sil_done[hc - 2])
                                tensor.wait_ge(vh.h, mul_done[hc - 2])
                            for dc in range(8):
                                inst = tensor.matmul(
                                    pg[b].ap(),
                                    wbf[b].ap()[:, dc * 128:(dc + 1) * 128],
                                    xf.ap()[:, dc * TPC + tt * 512:
                                            dc * TPC + tt * 512 + 512],
                                    start=(dc == 0), stop=(dc == 7))
                            inst.then_inc(te.h, 1)
                            for dc in range(8):
                                inst = tensor.matmul(
                                    pu[b].ap(),
                                    wbf[b].ap()[:, 1024 + dc * 128:
                                                1024 + (dc + 1) * 128],
                                    xf.ap()[:, dc * TPC + tt * 512:
                                            dc * TPC + tt * 512 + 512],
                                    start=(dc == 0), stop=(dc == 7))
                            inst.then_inc(te.h, 1)

                    @blk.scalar
                    def _(scalar):
                        for hc in range(NH):
                            b = hc % 2
                            scalar.wait_ge(te.h, mm_g[hc])
                            if hc >= 2:
                                scalar.wait_ge(vh.h, mul_done[hc - 2])
                            scalar.activation(gtmp[b].ap(), pg[b].ap(),
                                              AF.Silu).then_inc(se.h, 1)
                ld.n = loaded[-1]
                vc.n = casted[-1]
                te.n = mm_u[-1]
                se.n = sil_done[-1]
                vh.n = mul_done[-1]

            with p.scope():
                po = [p.psum("l3f_po", [128, 1024]) for _ in range(4)]
                loaded = [ld.n + 16 * (i + 1) for i in range(NH)]
                casted = [vc.n + (i + 1) for i in range(NH)]
                mm_done = [te.n + (i + 1) for i in range(NH)]
                with p.block() as blk:
                    @blk.sync
                    def _(sync):
                        for hc in range(NH):
                            b = hc % 2
                            if hc >= 2:
                                sync.wait_ge(vc.h, casted[hc - 2])
                            sync.dma_start(
                                out=wint[b].ap()[:, 0:1024],
                                in_=wdT.ap()[hc * 128:(hc + 1) * 128, :]
                            ).then_inc(ld.h, 16)

                    @blk.vector
                    def _(vector):
                        for hc in range(NH):
                            b = hc % 2
                            vector.wait_ge(ld.h, loaded[hc])
                            if hc >= 2:
                                vector.wait_ge(te.h, mm_done[hc - 2])
                            vector.tensor_scalar(
                                wbf[b].ap()[:, 0:1024], wint[b].ap()[:, 0:1024],
                                1.0 / 127.0, None, OP.mult).then_inc(vc.h, 1)

                    @blk.tensor
                    def _(tensor):
                        for hc in range(NH):
                            b = hc % 2
                            tensor.wait_ge(vc.h, casted[hc])
                            for st in range(4):
                                for half in range(2):
                                    inst = tensor.matmul(
                                        po[st].ap()[:, half * 512:(half + 1) * 512],
                                        hbuf.ap()[:, hc * 512 + st * 128:
                                                  hc * 512 + (st + 1) * 128],
                                        wbf[b].ap()[:, half * 512:(half + 1) * 512],
                                        start=(hc == 0), stop=(hc == NH - 1))
                            inst.then_inc(te.h, 1)
                ld.n = loaded[-1]
                vc.n = casted[-1]
                te.n = mm_done[-1]

                with p.block() as blk:
                    @blk.vector
                    def _(vector, tt=tt):
                        for st in range(4):
                            t = tt * 4 + st
                            vector.tensor_tensor(
                                xres.ap()[:, t * 1024:(t + 1) * 1024],
                                xres.ap()[:, t * 1024:(t + 1) * 1024],
                                po[st].ap(), OP.add)


class Fenced:

    def __init__(self, eng, sem):
        self.eng = eng
        self.sem = sem
        self.pending = None

    def __getattr__(self, name):
        fn = getattr(self.eng, name)

        def wrap(*a, **kw):
            if self.pending is not None:
                self.eng.wait_ge(self.sem.h, self.pending)
            inst = fn(*a, **kw)
            self.pending = self.sem.inc(inst, 1)
            return inst
        return wrap

    def wait_ge(self, h, n):
        self.eng.wait_ge(h, n)


class QuantSched:

    def __init__(self, p, jobs, scr):
        self.p = p
        self.jobs = jobs
        self.scr = scr
        self.qv = p.sem("qv")
        self.qs = p.sem("qs")
        self.qg = p.sem("qg")
        self.v0 = self.qv.n
        self.s0 = self.qs.n
        self.g0 = self.qg.n
        self.qv.n += 14 * len(jobs)
        self.qs.n += 2 * len(jobs)
        self.qg.n += len(jobs)

    def vector_ops(self, vector, extra_wait=None):
        scr, qv, qs, qg = self.scr, self.qv, self.qs, self.qg
        c = self.v0
        for i, (src, dst) in enumerate(self.jobs):
            if extra_wait is not None:
                extra_wait(vector, i)
            F = src.shape[-1]
            NB = F // 16
            seq = [
                lambda: vector.reduce_max(
                    scr["scl"].ap()[:, 0:NB],
                    src.rearrange("p (b k) -> p b k", k=16),
                    axis=AX.X, apply_absolute_value=True),
                lambda: vector.tensor_scalar(
                    scr["sclc"].ap()[:, 0:NB], scr["scl"].ap()[:, 0:NB],
                    1.0 / 6.0, 1e-12, OP.mult, OP.max),
                lambda: vector.reciprocal(
                    scr["rcp"].ap()[:, 0:NB], scr["sclc"].ap()[:, 0:NB]),
            ]
            for fn in seq:
                fn().then_inc(qv.h, 1)
                c += 1
                vector.wait_ge(qv.h, c)
            vector.wait_ge(qg.h, self.g0 + i + 1)
            vector.tensor_tensor(scr["xn"].ap()[:, 0:F], src,
                                 scr["rbig"].ap()[:, 0:F],
                                 OP.mult).then_inc(qv.h, 1)
            c += 1
            vector.wait_ge(qv.h, c)
            vector.wait_ge(qs.h, self.s0 + 2 * (i + 1))
            seq2 = [
                lambda: vector.tensor_scalar(
                    scr["i1"].ap()[:, 0:F], scr["xa"].ap()[:, 0:F],
                    2.0, 2.0, OP.min, OP.mult),
                lambda: vector.tensor_scalar(
                    scr["i2"].ap()[:, 0:F], scr["xa"].ap()[:, 0:F],
                    2.0, 4.0, OP.max, OP.min),
                lambda: vector.tensor_scalar(
                    scr["g5"].ap()[:, 0:F], scr["xa"].ap()[:, 0:F],
                    5.0, 2.0, OP.is_gt, OP.mult),
                lambda: vector.tensor_copy(
                    scr["f1"].ap()[:, 0:F], scr["i1"].ap()[:, 0:F]),
                lambda: vector.tensor_copy(
                    scr["f2"].ap()[:, 0:F], scr["i2"].ap()[:, 0:F]),
                lambda: vector.scalar_tensor_tensor(
                    scr["q1"].ap()[:, 0:F], scr["f1"].ap()[:, 0:F],
                    0.5, scr["f2"].ap()[:, 0:F], OP.mult, OP.add),
                lambda: vector.tensor_scalar(
                    scr["q1"].ap()[:, 0:F], scr["q1"].ap()[:, 0:F],
                    -2.0, None, OP.add),
                lambda: vector.tensor_tensor(
                    scr["q1"].ap()[:, 0:F], scr["q1"].ap()[:, 0:F],
                    scr["g5"].ap()[:, 0:F], OP.add),
                lambda: vector.tensor_tensor(
                    scr["q1"].ap()[:, 0:F], scr["q1"].ap()[:, 0:F],
                    scr["sg"].ap()[:, 0:F], OP.mult),
                lambda: vector.tensor_tensor(
                    dst, scr["q1"].ap()[:, 0:F],
                    scr["sbig"].ap()[:, 0:F], OP.mult),
            ]
            for fn in seq2:
                fn().then_inc(qv.h, 1)
                c += 1
                vector.wait_ge(qv.h, c)
        assert c == self.v0 + 14 * len(self.jobs)

    def gpsimd_ops(self, gpsimd):
        scr, qv, qg = self.scr, self.qv, self.qg
        for i, (src, dst) in enumerate(self.jobs):
            F = src.shape[-1]
            NB = F // 16
            gpsimd.wait_ge(qv.h, self.v0 + 14 * i + 3)
            for k in range(16):
                gpsimd.tensor_copy(
                    scr["rbig"].ap().rearrange("p (b k) -> p b k", k=16)
                    [:, 0:NB, k:k + 1].squeeze(2), scr["rcp"].ap()[:, 0:NB])
                inst = gpsimd.tensor_copy(
                    scr["sbig"].ap().rearrange("p (b k) -> p b k", k=16)
                    [:, 0:NB, k:k + 1].squeeze(2), scr["sclc"].ap()[:, 0:NB])
            inst.then_inc(qg.h, 1)

    def scalar_ops(self, scalar):
        scr, qv, qs = self.scr, self.qv, self.qs
        for i, (src, dst) in enumerate(self.jobs):
            F = src.shape[-1]
            scalar.wait_ge(qv.h, self.v0 + 14 * i + 4)
            scalar.activation(scr["xa"].ap()[:, 0:F], scr["xn"].ap()[:, 0:F],
                              AF.Abs).then_inc(qs.h, 1)
            scalar.activation(scr["sg"].ap()[:, 0:F], scr["xn"].ap()[:, 0:F],
                              AF.Sign).then_inc(qs.h, 1)


def quant_scratch(p, F=704):
    d = {
        "scl": p.sbuf("q_scl", [128, F // 16], F32),
        "sclc": p.sbuf("q_sclc", [128, F // 16], F32),
        "rcp": p.sbuf("q_rcp", [128, F // 16], F32),
        "rbig": p.sbuf("q_rbig", [128, F], F32),
        "sbig": p.sbuf("q_sbig", [128, F], F32),
        "xn": p.sbuf("q_xn", [128, F], F32),
        "xa": p.sbuf("q_xa", [128, F], F32),
        "sg": p.sbuf("q_sg", [128, F], F32),
        "i1": p.sbuf("q_i1", [128, F], I32),
        "i2": p.sbuf("q_i2", [128, F], I32),
        "g5": p.sbuf("q_g5", [128, F], F32),
    }
    d["f1"] = d["rbig"]
    d["f2"] = d["xn"]
    d["q1"] = d["xa"]
    return d



def prep_L2_ffn(core, inputs):
    return {
        "l2f_wg": np.ascontiguousarray(inputs["L2_ffn_w_gate"]),
        "l2f_wu": np.ascontiguousarray(inputs["L2_ffn_w_up"]),
        "l2f_wd": np.ascontiguousarray(inputs["L2_ffn_w_down"]),
    }


def emit_quant_inplace(p, jobs, scr):
    qs = QuantSched(p, jobs, scr)
    with p.block() as blk:
        @blk.vector
        def _(vector):
            qs.vector_ops(vector)

        @blk.scalar
        def _(scalar):
            qs.scalar_ops(scalar)

        @blk.gpsimd
        def _(gpsimd):
            qs.gpsimd_ops(gpsimd)


def emit_weight_quant_fp4(p, wsrc, wq_fm, R, C, scr):
    nc = p.nc
    NR = R // 128
    W = 704 if C % 704 == 0 else 512
    NCH = C // W
    ld = p.sem("wq_ld")
    st = p.sem("wq_st")
    with p.scope():
        wtile = [p.sbuf("wq_tile", [128, C], F32) for _ in range(2)]
        wqtm = p.sbuf("wq_tm", [128, C], BF16)
        tstage2 = [p.sbuf("wq_stage", [128, C], BF16) for _ in range(2)]
        with p.block() as blk:
            @blk.sync
            def _(sync):
                sync.dma_start(out=wtile[0].ap(),
                               in_=wsrc.ap()[0:128, :]).then_inc(ld.h, 16)
                ld.n += 16
        for rt in range(NR):
            b = rt % 2
            ld_tok = ld.n
            jobs = [(wtile[b].ap()[:, ch * W:(ch + 1) * W],
                     wqtm.ap()[:, ch * W:(ch + 1) * W]) for ch in range(NCH)]
            qsch = QuantSched(p, jobs, scr)
            with p.block() as blk:
                @blk.sync
                def _(sync, rt=rt):
                    if rt + 1 < NR:
                        sync.dma_start(
                            out=wtile[(rt + 1) % 2].ap(),
                            in_=wsrc.ap()[(rt + 1) * 128:(rt + 2) * 128, :]
                        ).then_inc(ld.h, 16)
                        ld.n += 16

                @blk.vector
                def _(vector):
                    vector.wait_ge(ld.h, ld_tok)
                    qsch.vector_ops(vector)

                @blk.scalar
                def _(scalar):
                    qsch.scalar_ops(scalar)

                @blk.gpsimd
                def _(gpsimd):
                    qsch.gpsimd_ops(gpsimd)
            tstage = tstage2[rt % 2]
            pairs = [(wqtm.ap()[:, dc * 128:(dc + 1) * 128],
                      tstage.ap()[:, dc * 128:(dc + 1) * 128])
                     for dc in range(C // 128)]
            emit_transpose_many(p, pairs)
            with p.block() as blk:
                @blk.gpsimd
                def _(gpsimd, rt=rt, tstage=tstage):
                    if rt >= 2:
                        gpsimd.wait_ge(st.h, st.n - 16)
                    gpsimd.dma_start(
                        out=wq_fm.ap()[:, rt * 128:(rt + 1) * 128]
                            .rearrange("(c p) r -> p c r", p=128),
                        in_=tstage.ap().rearrange("p (c r) -> p c r", r=128)
                    ).then_inc(st.h, 16)
                    st.n += 16
        with p.block() as blk:
            @blk.gpsimd
            def _(gpsimd):
                gpsimd.wait_ge(st.h, st.n)


def emit_L2_ffn(p):
    nc = p.nc
    I2 = 2816
    NI = I2 // 128
    wg = p.dram("l2f_wg", [I2, D], F32, kind="ExternalInput")
    wu = p.dram("l2f_wu", [I2, D], F32, kind="ExternalInput")
    wd = p.dram("l2f_wd", [D, I2], F32, kind="ExternalInput")
    wgq = p.dram("l2f_wgq", [D, I2], BF16)
    wuq = p.dram("l2f_wuq", [D, I2], BF16)
    wdqT = p.dram("l2f_wdqT", [I2, D], BF16)
    xres = p.sb["xres"]
    with p.scope():
        scr = quant_scratch(p, 704)
        emit_weight_quant_fp4(p, wg, wgq, I2, D, scr)
        emit_weight_quant_fp4(p, wu, wuq, I2, D, scr)
        emit_weight_quant_fp4(p, wd, wdqT, D, I2, scr)

        xqf = p.sbuf("l2f_xqf", [128, ND * TPC], BF16)
        with p.scope():
            xhat = p.sbuf("l2f_xhat", [128, NT * 1024], BF16)
            emit_pre_ln(p, xhat)
            jobs = []
            for t in range(NT):
                for ch in range(2):
                    ap = xhat.ap()[:, t * 1024 + ch * 512:
                                   t * 1024 + (ch + 1) * 512]
                    jobs.append((ap, ap))
            emit_quant_inplace(p, jobs, scr)
            emit_tm_to_fm(p, xhat, xqf)

        h_tm = p.sbuf("l2f_h", [128, 2 * I2], BF16)
        hq_fm = p.sbuf("l2f_hqfm", [128, NI * 256], BF16)
        wbg = p.sbuf("l2f_wbg", [128, 8 * 512], BF16)
        wbu = p.sbuf("l2f_wbu", [128, 8 * 512], BF16)
        wbd = p.sbuf("l2f_wbd", [128, 1024], BF16)
        gtmp = p.sbuf("l2f_gtmp", [128, 2 * 512], F32)
        ld = p.sem("l2f_ld")
        vc = p.sem("l2f_vc")
        te = p.sem("l2f_te")
        se = p.sem("l2f_se")
        cts = [(i * 512, min(512, I2 - i * 512)) for i in range(6)]
        scr512 = scr
        for tg in range(8):
            t0 = tg * 2
            with p.scope():
                pg = [p.psum("l2f_pg", [128, 512]) for _ in range(2)]
                pu = [p.psum("l2f_pu", [128, 512]) for _ in range(2)]
                ld0, te0, se0, vc0 = ld.n, te.n, se.n, vc.n
                ld.n += 32 * len(cts)
                te.n += 2 * len(cts)
                se.n += 2 * len(cts)
                vc.n += 2 * len(cts)
                with p.block() as blk:
                    @blk.sync
                    def _(sync):
                        for ci, (c0, cw) in enumerate(cts):
                            if ci >= 1:
                                sync.wait_ge(te.h, te0 + 2 * ci)
                            sync.dma_start(
                                out=wbg.ap().rearrange("p (c k) -> p c k", c=8)
                                    [:, :, 0:cw],
                                in_=wgq.ap()[:, c0:c0 + cw]
                                    .rearrange("(c p) k -> p c k", p=128)
                            ).then_inc(ld.h, 16)
                            sync.dma_start(
                                out=wbu.ap().rearrange("p (c k) -> p c k", c=8)
                                    [:, :, 0:cw],
                                in_=wuq.ap()[:, c0:c0 + cw]
                                    .rearrange("(c p) k -> p c k", p=128)
                            ).then_inc(ld.h, 16)

                    @blk.tensor
                    def _(tensor):
                        for ci, (c0, cw) in enumerate(cts):
                            tensor.wait_ge(ld.h, ld0 + 32 * (ci + 1))
                            if ci >= 1:
                                tensor.wait_ge(vc.h, vc0 + 2 * ci)
                            for t in range(2):
                                for dc in range(8):
                                    xl = xqf.ap()[:, dc * TPC + (t0 + t) * 128:
                                                  dc * TPC + (t0 + t + 1) * 128]
                                    inst = tensor.matmul(
                                        pg[t].ap()[:, 0:cw], xl,
                                        wbg.ap()[:, dc * 512:dc * 512 + cw],
                                        start=(dc == 0), stop=(dc == 7))
                            inst.then_inc(te.h, 1)
                            for t in range(2):
                                for dc in range(8):
                                    xl = xqf.ap()[:, dc * TPC + (t0 + t) * 128:
                                                  dc * TPC + (t0 + t + 1) * 128]
                                    inst = tensor.matmul(
                                        pu[t].ap()[:, 0:cw], xl,
                                        wbu.ap()[:, dc * 512:dc * 512 + cw],
                                        start=(dc == 0), stop=(dc == 7))
                            inst.then_inc(te.h, 1)

                    @blk.scalar
                    def _(scalar):
                        for ci, (c0, cw) in enumerate(cts):
                            scalar.wait_ge(te.h, te0 + 2 * ci + 1)
                            if ci >= 1:
                                scalar.wait_ge(vc.h, vc0 + 2 * ci)
                            for t in range(2):
                                inst = scalar.activation(
                                    gtmp.ap()[:, t * 512:t * 512 + cw],
                                    pg[t].ap()[:, 0:cw], AF.Silu)
                            inst.then_inc(se.h, 2)

                    @blk.vector
                    def _(vector):
                        for ci, (c0, cw) in enumerate(cts):
                            vector.wait_ge(te.h, te0 + 2 * (ci + 1))
                            vector.wait_ge(se.h, se0 + 2 * (ci + 1))
                            for t in range(2):
                                inst = vector.tensor_tensor(
                                    h_tm.ap()[:, t * I2 + c0:t * I2 + c0 + cw],
                                    gtmp.ap()[:, t * 512:t * 512 + cw],
                                    pu[t].ap()[:, 0:cw], OP.mult)
                            inst.then_inc(vc.h, 2)
            jobs = []
            for t in range(2):
                for ci, (c0, cw) in enumerate(cts):
                    jobs.append((h_tm.ap()[:, t * I2 + c0:t * I2 + c0 + cw],
                                 h_tm.ap()[:, t * I2 + c0:t * I2 + c0 + cw]))
            emit_quant_inplace(p, jobs, scr512)
            pairs = []
            for t in range(2):
                for ic in range(NI):
                    pairs.append((
                        h_tm.ap()[:, t * I2 + ic * 128:t * I2 + (ic + 1) * 128],
                        hq_fm.ap()[:, ic * 256 + t * 128:ic * 256 + (t + 1) * 128]))
            emit_transpose_many(p, pairs)
            with p.scope():
                po = [p.psum("l2f_po", [128, 1024]) for _ in range(2)]
                ld0, te0 = ld.n, te.n
                ld.n += 16 * NI
                te.n += NI
                with p.block() as blk:
                    @blk.sync
                    def _(sync):
                        for ic in range(NI):
                            if ic >= 1:
                                sync.wait_ge(te.h, te0 + ic)
                            sync.dma_start(
                                out=wbd.ap(),
                                in_=wdqT.ap()[ic * 128:(ic + 1) * 128, :]
                            ).then_inc(ld.h, 16)

                    @blk.tensor
                    def _(tensor):
                        for ic in range(NI):
                            tensor.wait_ge(ld.h, ld0 + 16 * (ic + 1))
                            for t in range(2):
                                for half in range(2):
                                    inst = tensor.matmul(
                                        po[t].ap()[:, half * 512:(half + 1) * 512],
                                        hq_fm.ap()[:, ic * 256 + t * 128:
                                                   ic * 256 + (t + 1) * 128],
                                        wbd.ap()[:, half * 512:(half + 1) * 512],
                                        start=(ic == 0), stop=(ic == NI - 1))
                            inst.then_inc(te.h, 1)
                with p.block() as blk:
                    @blk.vector
                    def _(vector, t0=t0):
                        for t in range(2):
                            tt = t0 + t
                            vector.tensor_tensor(
                                xres.ap()[:, tt * 1024:(tt + 1) * 1024],
                                xres.ap()[:, tt * 1024:(tt + 1) * 1024],
                                po[t].ap(), OP.add)





def prep_L1_ffn(core, inputs):
    gup = inputs["L1_ffn_gate_up_proj"]
    down = inputs["L1_ffn_down_proj"]
    return {
        "l1f_gwT": np.ascontiguousarray(inputs["L1_ffn_gate_weight"].T),
        "l1f_gupT": np.ascontiguousarray(np.transpose(gup, (0, 2, 1))),
        "l1f_downT": np.ascontiguousarray(np.transpose(down, (0, 2, 1))),
        "l1f_sim": np.ascontiguousarray(inputs["L1_ffn_similarity_matrix"]),
    }


def emit_moe_routing(p, xf, gwT_d, combine, gate_bias_d=None, norm_topk=False,
                     reroute_sim_d=None, route_pfx="r"):
    nc = p.nc
    E = 8
    with p.scope():
        gw = p.sbuf(route_pfx + "gw", [128, ND * E], BF16)
        probs = p.sbuf(route_pfx + "probs", [128, NT * E], F32)
        gm = p.sbuf(route_pfx + "gm", [128, NT], F32)
        lsum = p.sbuf(route_pfx + "lsum", [128, NT], F32)
        is1 = p.sbuf(route_pfx + "is1", [128, NT * E], BF16)
        is2 = p.sbuf(route_pfx + "is2", [128, NT * E], F32)
        b2 = p.sbuf(route_pfx + "b2", [128, NT * E], F32)
        biased = p.sbuf(route_pfx + "biased", [128, NT * E], F32)
        gbia = p.sbuf(route_pfx + "gbia", [128, E], F32) if gate_bias_d is not None else None
        ld = p.sem("rt_ld")
        with p.block() as blk:
            @blk.gpsimd
            def _(gpsimd):
                gpsimd.dma_start(
                    out=gw.ap().rearrange("p (c e) -> p c e", c=ND),
                    in_=gwT_d.ap().rearrange("(c p) e -> p c e", p=128)
                ).then_inc(ld.h, 16)
                ld.n += 16
        if gate_bias_d is not None:
            with p.scope():
                gb1 = p.sbuf(route_pfx + "gb1", [1, E], BF16)
                pb = p.psum(route_pfx + "pb", [128, E])
                with p.block() as blk:
                    @blk.gpsimd
                    def _(gpsimd):
                        gpsimd.dma_start(out=gb1.ap(), in_=gate_bias_d.ap()
                                       ).then_inc(ld.h, 16)
                        ld.n += 16
                with p.block() as blk:
                    @blk.vector
                    def _(vector):
                        vector.wait_ge(ld.h, ld.n)
                with p.block() as blk:
                    @blk.tensor
                    def _(tensor):
                        tensor.matmul(pb.ap(), p.sb["ones_one"].ap()[0:1, 0:128],
                                      gb1.ap(), start=True, stop=True)
                with p.block() as blk:
                    @blk.vector
                    def _(vector):
                        vector.tensor_copy(gbia.ap(), pb.ap())
        with p.scope():
            pl = [p.psum(route_pfx + "pl", [128, E]) for _ in range(4)]
            with p.block() as blk:
                @blk.vector
                def _(vector):
                    vector.wait_ge(ld.h, ld.n)
            te = p.sem("rt_te")
            ev = p.sem("rt_ev")
            te0, ev0 = te.n, ev.n
            te.n += NT
            ev.n += NT
            with p.block() as blk:
                @blk.tensor
                def _(tensor):
                    for t in range(NT):
                        if t >= 4:
                            tensor.wait_ge(ev.h, ev0 + t - 3)
                        for dc in range(ND):
                            inst = tensor.matmul(
                                pl[t % 4].ap(),
                                xf.ap()[:, dc * TPC + t * 128:dc * TPC + (t + 1) * 128],
                                gw.ap()[:, dc * E:(dc + 1) * E],
                                start=(dc == 0), stop=(dc == ND - 1))
                        inst.then_inc(te.h, 1)

                @blk.vector
                def _(vector):
                    for t in range(NT):
                        vector.wait_ge(te.h, te0 + t + 1)
                        vector.tensor_copy(probs.ap()[:, t * E:(t + 1) * E],
                                           pl[t % 4].ap()).then_inc(ev.h, 1)
        fv = p.sem("rt_fv")
        fs = p.sem("rt_fs")
        with p.block() as blk:
            @blk.vector
            def _(vector):
                grp = probs.ap().rearrange("p (t e) -> p t e", e=E)
                inst = vector.reduce_max(gm.ap(), grp, axis=AX.X)
                fv.n = fv.inc(inst, 1)
                vector.wait_ge(fv.h, fv.n)
                for t in range(NT):
                    inst = vector.tensor_scalar(
                        probs.ap()[:, t * E:(t + 1) * E],
                        probs.ap()[:, t * E:(t + 1) * E],
                        gm.ap()[:, t:t + 1], None, OP.subtract)
                fv.inc(inst, 1)

            @blk.scalar
            def _(scalar):
                scalar.wait_ge(fv.h, fv.n)
                scalar.activation(probs.ap(), probs.ap(), AF.Exp).then_inc(fs.h, 1)
                fs.n += 1

            @blk.vector
            def _(vector):
                vector.wait_ge(fs.h, fs.n)
                grp = probs.ap().rearrange("p (t e) -> p t e", e=E)
                inst = vector.reduce_sum(lsum.ap(), grp, axis=AX.X)
                fv.inc(inst, 1)
                vector.wait_ge(fv.h, fv.n)
                inst = vector.reciprocal(lsum.ap(), lsum.ap())
                fv.inc(inst, 1)
                vector.wait_ge(fv.h, fv.n)
                for t in range(NT):
                    inst = vector.tensor_scalar(
                        probs.ap()[:, t * E:(t + 1) * E],
                        probs.ap()[:, t * E:(t + 1) * E],
                        lsum.ap()[:, t:t + 1], None, OP.mult)
                fv.inc(inst, 1)
                vector.wait_ge(fv.h, fv.n)
                if gbia is not None:
                    for t in range(NT):
                        inst = vector.tensor_tensor(
                            biased.ap()[:, t * E:(t + 1) * E],
                            probs.ap()[:, t * E:(t + 1) * E], gbia.ap(), OP.add)
                else:
                    inst = vector.tensor_copy(biased.ap(), probs.ap())
                fv.inc(inst, 1)
                vector.wait_ge(fv.h, fv.n)
                bgrp = biased.ap().rearrange("p (t e) -> p t e", e=E)
                inst = vector.reduce_max(gm.ap(), bgrp, axis=AX.X)
                fv.inc(inst, 1)
                vector.wait_ge(fv.h, fv.n)
                for t in range(NT):
                    inst = vector.tensor_scalar(
                        is1.ap()[:, t * E:(t + 1) * E],
                        biased.ap()[:, t * E:(t + 1) * E],
                        gm.ap()[:, t:t + 1], None, OP.is_ge)
                fv.inc(inst, 1)
                vector.wait_ge(fv.h, fv.n)
                for t in range(NT):
                    inst = vector.scalar_tensor_tensor(
                        b2.ap()[:, t * E:(t + 1) * E],
                        is1.ap()[:, t * E:(t + 1) * E], -1e30,
                        biased.ap()[:, t * E:(t + 1) * E], OP.mult, OP.add)
                fv.inc(inst, 1)
                vector.wait_ge(fv.h, fv.n)
                b2grp = b2.ap().rearrange("p (t e) -> p t e", e=E)
                inst = vector.reduce_max(gm.ap(), b2grp, axis=AX.X)
                fv.inc(inst, 1)
                vector.wait_ge(fv.h, fv.n)
                for t in range(NT):
                    inst = vector.tensor_scalar(
                        is2.ap()[:, t * E:(t + 1) * E],
                        b2.ap()[:, t * E:(t + 1) * E],
                        gm.ap()[:, t:t + 1], None, OP.is_ge)
                fv.inc(inst, 1)
                vector.wait_ge(fv.h, fv.n)
                inst = vector.tensor_tensor(is2.ap(), is2.ap(), is1.ap(), OP.add)
                fv.inc(inst, 1)
                vector.wait_ge(fv.h, fv.n)
                inst = vector.tensor_tensor(combine.ap(), probs.ap(), is2.ap(),
                                            OP.mult)
                fv.inc(inst, 1)
                vector.wait_ge(fv.h, fv.n)
                if norm_topk:
                    cgrp = combine.ap().rearrange("p (t e) -> p t e", e=E)
                    inst = vector.reduce_sum(gm.ap(), cgrp, axis=AX.X)
                    fv.inc(inst, 1)
                    vector.wait_ge(fv.h, fv.n)
                    inst = vector.tensor_scalar(gm.ap(), gm.ap(), 1e-12, None,
                                                OP.max)
                    fv.inc(inst, 1)
                    vector.wait_ge(fv.h, fv.n)
                    inst = vector.reciprocal(gm.ap(), gm.ap())
                    fv.inc(inst, 1)
                    vector.wait_ge(fv.h, fv.n)
                    for t in range(NT):
                        inst = vector.tensor_scalar(
                            combine.ap()[:, t * E:(t + 1) * E],
                            combine.ap()[:, t * E:(t + 1) * E],
                            gm.ap()[:, t:t + 1], None, OP.mult)
                    fv.inc(inst, 1)
        if reroute_sim_d is not None:
            emit_sim_reroute(p, is1, combine, reroute_sim_d, route_pfx)


def emit_sim_reroute(p, is1, combine, sim_d, pfx):
    nc = p.nc
    E = 8
    ident = p.sb["ident"]
    with p.scope():
        cnt_sb = p.sbuf(pfx + "cnt", [8, 1], F32)
        g88 = p.sbuf(pfx + "g88", [8, 8], F32)
        tot = p.sbuf(pfx + "tot", [8, 1], F32)
        pmask = p.sbuf(pfx + "pmask", [8, 1], BF16)
        pmrow = p.sbuf(pfx + "pmrow", [1, 8], BF16)
        pmb = p.sbuf(pfx + "pmb", [8, 8], I32)
        sim_sb = p.sbuf(pfx + "sim", [8, 8], F32)
        neg30 = p.sbuf(pfx + "neg30", [8, 8], F32)
        masked = p.sbuf(pfx + "masked", [8, 8], F32)
        bsim = p.sbuf(pfx + "bsim", [8, 1], F32)
        ge8 = p.sbuf(pfx + "ge8", [8, 8], BF16)
        geT = p.sbuf(pfx + "geT", [8, 8], BF16)
        cume = p.sbuf(pfx + "cume", [8, 8], F32)
        first = p.sbuf(pfx + "first", [8, 8], F32)
        cond = p.sbuf(pfx + "cond", [8, 1], F32)
        condb = p.sbuf(pfx + "condb", [8, 8], I32)
        m8 = p.sbuf(pfx + "m8", [8, 8], BF16)
        triu8 = p.sbuf(pfx + "triu8", [8, 8], BF16)
        ones88 = p.sbuf(pfx + "ones88", [8, 8], F32)
        cwT = p.sbuf(pfx + "cwT", [8, 128], BF16)
        identf = p.sbuf(pfx + "identf", [128, 128], F32)
        idf8 = p.sbuf(pfx + "idf8", [8, 8], F32)
        ld = p.sem("sr_ld")
        st = p.sem("sr_st")
        cc = p.sem("sr_cc")
        fv = p.sem("sr_fv")
        ft = p.sem("sr_ft")
        cin = p.dram("cc_pm_in", [8, 1], F32)
        cout = p.dram("cc_pm_out", [64, 1], F32, addr_space="Shared")
        with p.block() as blk:
            @blk.vector
            def _(vector):
                vector.tensor_copy(identf.ap(), p.sb["ident"].ap())
                vector.tensor_copy(idf8.ap(), p.sb["ident"].ap()[0:8, 0:8])
        with p.scope():
            pc = p.psum(pfx + "pc", [8, 8])
            with p.block() as blk:
                @blk.tensor
                def _(tensor):
                    for t in range(NT):
                        inst = tensor.matmul(
                            pc.ap()[:, 0:1], is1.ap()[:, t * 8:(t + 1) * 8],
                            p.sb["ones_col"].ap(),
                            start=(t == 0), stop=(t == NT - 1))
                    inst.then_inc(ft.h, 1)
                    ft.n += 1

                @blk.vector
                def _(vector):
                    vector.wait_ge(ft.h, ft.n)
                    vector.tensor_copy(cnt_sb.ap(), pc.ap()[0:8, 0:1])
        with p.block() as blk:
            @blk.sync
            def _(sync):
                sync.dma_start(out=cin.ap(), in_=cnt_sb.ap()).then_inc(st.h, 16)
                st.n += 16
        with p.block() as blk:
            tok = st.n

            @blk.gpsimd
            def _(gpsimd):
                gpsimd.wait_ge(st.h, tok)
                cc.inc(gpsimd.collective_compute(
                    "AllGather", OP.bypass, replica_groups=[list(range(8))],
                    ins=[cin.ap().opt()], outs=[cout.ap().opt()]))
        with p.block() as blk:
            tok = cc.n

            @blk.sync
            def _(sync):
                sync.wait_ge(cc.h, tok)
                sync.dma_start(out=g88.ap(),
                               in_=cout.ap().rearrange("(r e) one -> e (r one)", e=8),
                               allow_slow_non_contiguous=True
                               ).then_inc(ld.h, 16)
                ld.n += 16
            @blk.vector
            def _(vector):
                vector.wait_ge(ld.h, ld.n)
        with p.block() as blk:
            @blk.sync
            def _(sync):
                sync.dma_start(out=sim_sb.ap(), in_=sim_d.ap()).then_inc(ld.h, 16)
                ld.n += 16

            @blk.gpsimd
            def _(gpsimd):
                gpsimd.memset(neg30.ap(), -1e30)
                gpsimd.memset(ones88.ap(), 1.0)

            @blk.vector
            def _(vector):
                vector.wait_ge(ld.h, ld.n)
                inst = vector.reduce_sum(tot.ap(), g88.ap(), axis=AX.X)
                fv.inc(inst, 1)
                vector.wait_ge(fv.h, fv.n)
                inst = vector.tensor_scalar(pmask.ap(), tot.ap(), 0.0, None,
                                            OP.is_gt)
                fv.inc(inst, 1)
        with p.scope():
            pt = p.psum(pfx + "pt", [1, 8], BF16)
            pbb = p.psum(pfx + "pbb", [8, 8])
            with p.block() as blk:
                @blk.tensor
                def _(tensor):
                    tensor.transpose(pt.ap(), pmask.ap(), ident.ap()[0:8, 0:8])
            with p.block() as blk:
                @blk.vector
                def _(vector):
                    vector.tensor_copy(pmrow.ap(), pt.ap())
            with p.block() as blk:
                @blk.tensor
                def _(tensor):
                    tensor.matmul(pbb.ap(), p.sb["ones_one"].ap()[0:1, 0:8],
                                  pmrow.ap(), start=True, stop=True)
            with p.block() as blk:
                @blk.vector
                def _(vector):
                    vector.tensor_copy(pmb.ap(), pbb.ap())
        with p.block() as blk:
            @blk.vector
            def _(vector):
                inst = vector.select(masked.ap(), pmb.ap(), sim_sb.ap(),
                                     neg30.ap())
                fv.inc(inst, 1)
                vector.wait_ge(fv.h, fv.n)
                inst = vector.reduce_max(bsim.ap(), masked.ap(), axis=AX.X)
                fv.inc(inst, 1)
                vector.wait_ge(fv.h, fv.n)
                inst = vector.tensor_scalar(ge8.ap(), masked.ap(), bsim.ap(),
                                            None, OP.is_ge)
                fv.inc(inst, 1)
        with p.scope():
            ptT = p.psum(pfx + "ptT", [8, 8], BF16)
            pcum = p.psum(pfx + "pcum", [8, 8])
            with p.block() as blk:
                @blk.vector
                def _(vector):
                    vector.tensor_tensor(triu8.ap(), p.sb["iota_p"].ap()[0:8, 0:8],
                                         p.sb["iota_f"].ap()[0:8, 0:8], OP.is_lt)
            with p.block() as blk:
                @blk.tensor
                def _(tensor):
                    tensor.transpose(ptT.ap(), ge8.ap(), ident.ap()[0:8, 0:8])
            with p.block() as blk:
                @blk.vector
                def _(vector):
                    vector.tensor_copy(geT.ap(), ptT.ap())
            with p.block() as blk:
                @blk.tensor
                def _(tensor):
                    tensor.matmul(pcum.ap(), geT.ap(), triu8.ap(),
                                  start=True, stop=True)
            with p.block() as blk:
                @blk.vector
                def _(vector):
                    inst = vector.tensor_scalar(cume.ap(), pcum.ap(), 0.5, None,
                                                OP.is_lt)
                    fv.inc(inst, 1)
                    vector.wait_ge(fv.h, fv.n)
                    inst = vector.tensor_tensor(first.ap(), cume.ap(), ge8.ap(),
                                                OP.mult)
                    fv.inc(inst, 1)
                    vector.wait_ge(fv.h, fv.n)
                    inst = vector.tensor_scalar(cond.ap(), bsim.ap(), 0.7, None,
                                                OP.is_ge)
                    fv.inc(inst, 1)
                    vector.wait_ge(fv.h, fv.n)
                    inst = vector.scalar_tensor_tensor(
                        bsim.ap(), pmask.ap(), -1.0, cond.ap()
                        , OP.mult, OP.mult)
                    fv.inc(inst, 1)
                    vector.wait_ge(fv.h, fv.n)
                    inst = vector.tensor_tensor(cond.ap(), cond.ap(),
                                                bsim.ap(), OP.add)
                    fv.inc(inst, 1)
                    vector.wait_ge(fv.h, fv.n)
                    inst = vector.tensor_scalar(condb.ap(), ones88.ap(),
                                                cond.ap(), None, OP.mult)
                    fv.inc(inst, 1)
                    vector.wait_ge(fv.h, fv.n)
                    inst = vector.select(m8.ap(), condb.ap(), first.ap(),
                                         idf8.ap())
                    fv.inc(inst, 1)
        with p.scope():
            ptr = [p.psum(pfx + "ctr", [8, 128], F32) for _ in range(2)]
            pcm = [p.psum(pfx + "pcm", [128, 8]) for _ in range(2)]
            cwTs = [p.sbuf(pfx + "cwTs", [8, 128], BF16) for _ in range(2)]
            te = p.sem("sr_te")
            ev = p.sem("sr_ev")
            te0, ev0 = te.n, ev.n
            with p.block() as blk:
                @blk.tensor
                def _(tensor):
                    for t in range(NT):
                        b = t % 2
                        if t >= 2:
                            tensor.wait_ge(ev.h, ev0 + 3 * (t - 1))
                        tensor.transpose(ptr[b].ap(),
                                         combine.ap()[:, t * 8:(t + 1) * 8]
                                         , identf.ap()).then_inc(te.h, 1)
                        tensor.wait_ge(ev.h, ev0 + 3 * t + 1)
                        tensor.matmul(pcm[b].ap(), cwTs[b].ap(), m8.ap(),
                                      start=True, stop=True).then_inc(te.h, 1)
                    te.n += 2 * NT

                @blk.vector
                def _(vector):
                    for t in range(NT):
                        b = t % 2
                        vector.wait_ge(te.h, te0 + 2 * t + 1)
                        vector.tensor_copy(cwTs[b].ap(), ptr[b].ap()
                                           ).then_inc(ev.h, 1)
                        vector.wait_ge(te.h, te0 + 2 * t + 2)
                        vector.tensor_copy(combine.ap()[:, t * 8:(t + 1) * 8],
                                           pcm[b].ap()).then_inc(ev.h, 2)
                    ev.n += 3 * NT


def emit_L1_ffn(p):
    nc = p.nc
    E = 8
    gwT = p.dram("l1f_gwT", [D, E], F32, kind="ExternalInput")
    gupT = p.dram("l1f_gupT", [E, D, 2048], F32, kind="ExternalInput")
    downT = p.dram("l1f_downT", [E, D, D], F32, kind="ExternalInput")
    sim_d = p.dram("l1f_sim", [E, E], F32, kind="ExternalInput")
    xres = p.sb["xres"]
    with p.scope():
        xf = p.sbuf("l1f_xf", [128, ND * TPC], BF16)
        with p.scope():
            xhat = p.sbuf("l1f_xhat", [128, NT * 1024], BF16)
            emit_pre_ln(p, xhat)
            emit_tm_to_fm(p, xhat, xf)
        combine = p.sbuf("l1f_combine", [128, NT * E], F32)
        emit_moe_routing(p, xf, gwT, combine, norm_topk=True,
                         reroute_sim_d=sim_d, route_pfx="l1f_")
        wgup = p.sbuf("l1f_wgup", [128, ND * 2048], BF16)
        wdn = p.sbuf("l1f_wdn", [128, ND * 1024], BF16)
        hm = p.sbuf("l1f_hm", [128, 8 * 512], BF16)
        gtmp = p.sbuf("l1f_gtmp", [128, 512], F32)
        ld = p.sem("l1f_ld")
        te = p.sem("l1f_te")
        se = p.sem("l1f_se")
        vc = p.sem("l1f_vc")
        for e in range(E):
            ld_tok0 = ld.n
            with p.block() as blk:
                @blk.gpsimd
                def _(gpsimd, e=e):
                    gpsimd.dma_start(
                        out=wgup.ap().rearrange("p (c k) -> p c k", c=ND),
                        in_=gupT.ap()[e].rearrange("(c p) k -> p c k", p=128)
                    ).then_inc(ld.h, 16)
                    gpsimd.dma_start(
                        out=wdn.ap().rearrange("p (c k) -> p c k", c=ND),
                        in_=downT.ap()[e].rearrange("(c p) k -> p c k", p=128)
                    ).then_inc(ld.h, 16)
                    ld.n += 32
            for tg in range(4):
                t0 = tg * 4
                with p.scope():
                    pg = p.psum("l1f_pg", [128, 512])
                    pu = p.psum("l1f_pu", [128, 512])
                    te0, se0, vc0 = te.n, se.n, vc.n
                    te.n += 2 * 8
                    se.n += 8
                    vc.n += 8
                    with p.block() as blk:
                        @blk.tensor
                        def _(tensor, t0=t0):
                            tensor.wait_ge(ld.h, ld_tok0 + 32)
                            for fc in range(8):
                                if fc >= 1:
                                    tensor.wait_ge(se.h, se0 + fc)
                                    tensor.wait_ge(vc.h, vc0 + fc)
                                for dc in range(ND):
                                    inst = tensor.matmul(
                                        pg.ap(),
                                        wgup.ap()[:, dc * 2048 + fc * 128:
                                                  dc * 2048 + (fc + 1) * 128],
                                        xf.ap()[:, dc * TPC + t0 * 128:
                                                dc * TPC + (t0 + 4) * 128],
                                        start=(dc == 0), stop=(dc == ND - 1))
                                inst.then_inc(te.h, 1)
                                for dc in range(ND):
                                    inst = tensor.matmul(
                                        pu.ap(),
                                        wgup.ap()[:, dc * 2048 + 1024 + fc * 128:
                                                  dc * 2048 + 1024 + (fc + 1) * 128],
                                        xf.ap()[:, dc * TPC + t0 * 128:
                                                dc * TPC + (t0 + 4) * 128],
                                        start=(dc == 0), stop=(dc == ND - 1))
                                inst.then_inc(te.h, 1)

                        @blk.scalar
                        def _(scalar):
                            for fc in range(8):
                                scalar.wait_ge(te.h, te0 + 2 * fc + 1)
                                if fc >= 1:
                                    scalar.wait_ge(vc.h, vc0 + fc)
                                scalar.activation(gtmp.ap(), pg.ap(),
                                                  AF.Silu).then_inc(se.h, 1)

                        @blk.vector
                        def _(vector):
                            for fc in range(8):
                                vector.wait_ge(te.h, te0 + 2 * fc + 2)
                                vector.wait_ge(se.h, se0 + fc + 1)
                                vector.tensor_tensor(
                                    hm.ap()[:, fc * 512:(fc + 1) * 512],
                                    gtmp.ap(), pu.ap(), OP.mult
                                ).then_inc(vc.h, 1)
                with p.scope():
                    po = [p.psum("l1f_po", [128, 1024]) for _ in range(4)]
                    te0 = te.n
                    te.n += 8
                    with p.block() as blk:
                        @blk.tensor
                        def _(tensor):
                            for fc in range(8):
                                for t in range(4):
                                    for half in range(2):
                                        inst = tensor.matmul(
                                            po[t].ap()[:, half * 512:(half + 1) * 512],
                                            hm.ap()[:, fc * 512 + t * 128:
                                                    fc * 512 + (t + 1) * 128],
                                            wdn.ap()[:, fc * 1024 + half * 512:
                                                     fc * 1024 + (half + 1) * 512],
                                            start=(fc == 0), stop=(fc == 7))
                                inst.then_inc(te.h, 1)
                    with p.block() as blk:
                        @blk.vector
                        def _(vector, t0=t0, e=e):
                            for t in range(4):
                                tt = t0 + t
                                vector.scalar_tensor_tensor(
                                    xres.ap()[:, tt * 1024:(tt + 1) * 1024],
                                    po[t].ap(),
                                    combine.ap()[:, tt * 8 + e:tt * 8 + e + 1],
                                    xres.ap()[:, tt * 1024:(tt + 1) * 1024],
                                    OP.mult, OP.add)



def prep_L0_ffn(core, inputs):
    perm = np.concatenate([np.arange(0, 2048, 2), np.arange(1, 2048, 2)])
    w1 = inputs["L0_ffn_w1"]; w3 = inputs["L0_ffn_w3"]; w2 = inputs["L0_ffn_w2"]
    out = {
        "l0f_gwT": np.ascontiguousarray(inputs["L0_ffn_gate_w"].T),
        "l0f_gb": np.ascontiguousarray(inputs["L0_ffn_gate_b"][None, :]),
        "l0f_w1p": np.ascontiguousarray(np.transpose(w1, (0, 2, 1))[:, :, perm]),
        "l0f_w3p": np.ascontiguousarray(np.transpose(w3, (0, 2, 1))[:, :, perm]),
        "l0f_w2T": np.ascontiguousarray(np.transpose(w2, (0, 2, 1))),
        "l0f_b1p": np.ascontiguousarray(inputs["L0_ffn_b1"][:, perm].reshape(8, 16, 128).transpose(0, 2, 1)),
        "l0f_b3p": np.ascontiguousarray(inputs["L0_ffn_b3"][:, perm].reshape(8, 16, 128).transpose(0, 2, 1)),
        "l0f_b2": np.ascontiguousarray(inputs["L0_ffn_b2"]),
        "l0f_sw1p": np.ascontiguousarray(inputs["L0_ffn_sw1"].T[:, perm]),
        "l0f_sw3p": np.ascontiguousarray(inputs["L0_ffn_sw3"].T[:, perm]),
        "l0f_sw2T": np.ascontiguousarray(inputs["L0_ffn_sw2"].T),
        "l0f_sb1p": np.ascontiguousarray(inputs["L0_ffn_sb1"][perm].reshape(16, 128).T[None]),
        "l0f_sb3p": np.ascontiguousarray(inputs["L0_ffn_sb3"][perm].reshape(16, 128).T[None]),
        "l0f_sb2": np.ascontiguousarray(inputs["L0_ffn_sb2"][None, :]),
    }
    return out


def emit_L0_ffn(p):
    nc = p.nc
    E = 8
    gwT = p.dram("l0f_gwT", [D, E], F32, kind="ExternalInput")
    gb_d = p.dram("l0f_gb", [1, E], F32, kind="ExternalInput")
    w1p = p.dram("l0f_w1p", [E, D, 2048], F32, kind="ExternalInput")
    w3p = p.dram("l0f_w3p", [E, D, 2048], F32, kind="ExternalInput")
    w2T = p.dram("l0f_w2T", [E, D, D], F32, kind="ExternalInput")
    b1p = p.dram("l0f_b1p", [E, 128, 16], F32, kind="ExternalInput")
    b3p = p.dram("l0f_b3p", [E, 128, 16], F32, kind="ExternalInput")
    b2_d = p.dram("l0f_b2", [E, D], F32, kind="ExternalInput")
    sw1p = p.dram("l0f_sw1p", [D, 2048], F32, kind="ExternalInput")
    sw3p = p.dram("l0f_sw3p", [D, 2048], F32, kind="ExternalInput")
    sw2T = p.dram("l0f_sw2T", [D, D], F32, kind="ExternalInput")
    sb1p = p.dram("l0f_sb1p", [1, 128, 16], F32, kind="ExternalInput")
    sb3p = p.dram("l0f_sb3p", [1, 128, 16], F32, kind="ExternalInput")
    sb2_d = p.dram("l0f_sb2", [1, D], F32, kind="ExternalInput")
    xres = p.sb["xres"]
    ALPHA = 1.702
    with p.scope():
        xf = p.sbuf("l0f_xf", [128, ND * TPC], BF16)
        with p.scope():
            xhat = p.sbuf("l0f_xhat", [128, NT * 1024], BF16)
            emit_pre_ln(p, xhat)
            emit_tm_to_fm(p, xhat, xf)
        combine = p.sbuf("l0f_combine", [128, NT * E], F32)
        emit_moe_routing(p, xf, gwT, combine, gate_bias_d=gb_d,
                         norm_topk=False, route_pfx="l0f_")
        w1b = p.sbuf("l0f_w1b", [128, ND * 2048], BF16)
        w3b = p.sbuf("l0f_w3b", [128, ND * 2048], BF16)
        w2b = p.sbuf("l0f_w2b", [128, ND * 1024], BF16)
        b1sb = p.sbuf("l0f_b1sb", [128, 16], F32)
        b3sb = p.sbuf("l0f_b3sb", [128, 16], F32)
        b2sb = p.sbuf("l0f_b2sb", [1, 1024], BF16)
        afull = p.sbuf("l0f_afull", [128, 8 * 512], BF16)
        u1 = p.sbuf("l0f_u1", [128, 2 * 512], F32)
        t1 = p.sbuf("l0f_t1", [128, 2 * 512], F32)
        g1 = p.sbuf("l0f_g1", [128, 512], F32)
        hg = p.sbuf("l0f_hg", [128, 512], F32)
        hl = p.sbuf("l0f_hl", [128, 512], F32)
        pg2 = p.sbuf("l0f_pg2", [128, 512], F32)
        sgb = hg
        lc = hl
        ld = p.sem("l0f_ld")
        te = p.sem("l0f_te")
        se = p.sem("l0f_se")
        vc = p.sem("l0f_vc")
        gc = p.sem("l0f_gc")
        for e in range(E + 1):
            shared = (e == E)
            ld_tok0 = ld.n
            with p.block() as blk:
                @blk.gpsimd
                def _(gpsimd, e=e, shared=shared):
                    src1 = sw1p.ap() if shared else w1p.ap()[e]
                    src3 = sw3p.ap() if shared else w3p.ap()[e]
                    src2 = sw2T.ap() if shared else w2T.ap()[e]
                    srcb1 = sb1p.ap()[0] if shared else b1p.ap()[e]
                    srcb3 = sb3p.ap()[0] if shared else b3p.ap()[e]
                    srcb2 = sb2_d.ap() if shared else b2_d.ap()[e:e + 1, :]
                    gpsimd.dma_start(
                        out=w1b.ap().rearrange("p (c k) -> p c k", c=ND),
                        in_=src1.rearrange("(c p) k -> p c k", p=128)
                    ).then_inc(ld.h, 16)
                    gpsimd.dma_start(
                        out=w3b.ap().rearrange("p (c k) -> p c k", c=ND),
                        in_=src3.rearrange("(c p) k -> p c k", p=128)
                    ).then_inc(ld.h, 16)
                    gpsimd.dma_start(
                        out=w2b.ap().rearrange("p (c k) -> p c k", c=ND),
                        in_=src2.rearrange("(c p) k -> p c k", p=128)
                    ).then_inc(ld.h, 16)
                    gpsimd.dma_start(out=b1sb.ap(), in_=srcb1
                                     ).then_inc(ld.h, 16)
                    gpsimd.dma_start(out=b3sb.ap(), in_=srcb3
                                     ).then_inc(ld.h, 16)
                    gpsimd.dma_start(out=b2sb.ap(), in_=srcb2
                                     ).then_inc(ld.h, 16)
                    ld.n += 96
            for tg in range(4):
                t0 = tg * 4
                with p.scope():
                    ph = [p.psum("l0f_ph", [128, 512]) for _ in range(4)]
                    te0, se0, vc0, gc0 = te.n, se.n, vc.n, gc.n
                    te.n += 4 * 8
                    se.n += 5 * 8
                    vc.n += 4 * 8
                    gc.n += 2 * 8
                    with p.block() as blk:
                        @blk.tensor
                        def _(tensor, t0=t0):
                            tensor.wait_ge(ld.h, ld_tok0 + 96)
                            for fc in range(8):
                                if fc >= 1:
                                    tensor.wait_ge(se.h, se0 + 5 * (fc - 1) + 4)
                                for k, (wsb, col) in enumerate(
                                        [(w1b, fc), (w3b, fc),
                                         (w1b, 8 + fc), (w3b, 8 + fc)]):
                                    for dc in range(ND):
                                        inst = tensor.matmul(
                                            ph[k].ap(),
                                            wsb.ap()[:, dc * 2048 + col * 128:
                                                     dc * 2048 + (col + 1) * 128],
                                            xf.ap()[:, dc * TPC + t0 * 128:
                                                    dc * TPC + (t0 + 4) * 128],
                                            start=(dc == 0), stop=(dc == ND - 1))
                                    inst.then_inc(te.h, 1)

                        @blk.scalar
                        def _(scalar):
                            for fc in range(8):
                                scalar.wait_ge(te.h, te0 + 4 * fc + 2)
                                if fc >= 1:
                                    scalar.wait_ge(vc.h, vc0 + 4 * (fc - 1) + 1)
                                    scalar.wait_ge(gc.h, gc0 + 2 * (fc - 1) + 1)
                                scalar.activation(u1.ap()[:, 0:512], ph[0].ap(),
                                                  AF.Identity,
                                                  bias=b1sb.ap()[:, fc:fc + 1]
                                                  ).then_inc(se.h, 1)
                                scalar.activation(t1.ap()[:, 0:512], ph[1].ap(),
                                                  AF.Identity,
                                                  bias=b3sb.ap()[:, fc:fc + 1]
                                                  ).then_inc(se.h, 1)
                                scalar.wait_ge(te.h, te0 + 4 * fc + 4)
                                scalar.activation(u1.ap()[:, 512:1024], ph[2].ap(),
                                                  AF.Identity,
                                                  bias=b1sb.ap()[:, 8 + fc:9 + fc]
                                                  ).then_inc(se.h, 1)
                                scalar.activation(t1.ap()[:, 512:1024], ph[3].ap(),
                                                  AF.Identity,
                                                  bias=b3sb.ap()[:, 8 + fc:9 + fc]
                                                  ).then_inc(se.h, 1)
                                scalar.wait_ge(vc.h, vc0 + 4 * fc + 2)
                                scalar.activation(sgb.ap(), g1.ap(), AF.Sigmoid,
                                                  scale=ALPHA).then_inc(se.h, 1)

                        @blk.vector
                        def _(vector):
                            for fc in range(8):
                                vector.wait_ge(se.h, se0 + 5 * fc + 2)
                                vector.tensor_tensor(hg.ap(), u1.ap()[:, 0:512],
                                                     t1.ap()[:, 0:512], OP.mult
                                                     ).then_inc(vc.h, 1)
                                vector.wait_ge(vc.h, vc0 + 4 * fc + 1)
                                vector.tensor_scalar(g1.ap(), hg.ap(), 7.0, None,
                                                     OP.min).then_inc(vc.h, 1)
                                vector.wait_ge(se.h, se0 + 5 * fc + 5)
                                vector.tensor_tensor(pg2.ap(), g1.ap(), sgb.ap(),
                                                     OP.mult).then_inc(vc.h, 1)
                                vector.wait_ge(vc.h, vc0 + 4 * fc + 3)
                                vector.wait_ge(gc.h, gc0 + 2 * fc + 2)
                                vector.scalar_tensor_tensor(
                                    afull.ap()[:, fc * 512:(fc + 1) * 512],
                                    lc.ap(), 1.0, pg2.ap(), OP.add, OP.mult
                                ).then_inc(vc.h, 1)

                        @blk.gpsimd
                        def _(gpsimd):
                            for fc in range(8):
                                gpsimd.wait_ge(se.h, se0 + 5 * fc + 4)
                                gpsimd.tensor_tensor(hl.ap(), u1.ap()[:, 512:1024],
                                                     t1.ap()[:, 512:1024], OP.mult
                                                     ).then_inc(gc.h, 1)
                                gpsimd.wait_ge(gc.h, gc0 + 2 * fc + 1)
                                gpsimd.tensor_scalar(lc.ap(), hl.ap(), 7.0, -7.0,
                                                     OP.min, OP.max
                                                     ).then_inc(gc.h, 1)
                with p.scope():
                    po = [p.psum("l0f_po", [128, 512]) for _ in range(4)]
                    for half in range(2):
                        te0 = te.n
                        te.n += 1
                        with p.block() as blk:
                            @blk.tensor
                            def _(tensor, half=half, t0=t0):
                                for t in range(4):
                                    inst = tensor.matmul(
                                        po[t].ap(), p.sb["ones_one"].ap(),
                                        b2sb.ap()[0:1, half * 512:(half + 1) * 512],
                                        start=True, stop=False)
                                for fc in range(8):
                                    for t in range(4):
                                        inst = tensor.matmul(
                                            po[t].ap(),
                                            afull.ap()[:, fc * 512 + t * 128:
                                                       fc * 512 + (t + 1) * 128],
                                            w2b.ap()[:, fc * 1024 + half * 512:
                                                     fc * 1024 + (half + 1) * 512],
                                            start=False, stop=(fc == 7))
                                inst.then_inc(te.h, 1)
                            tok = te.n

                            @blk.vector
                            def _(vector, half=half, t0=t0, e=e, shared=shared):
                                vector.wait_ge(te.h, tok)
                                for t in range(4):
                                    tt = t0 + t
                                    dst = xres.ap()[:, tt * 1024 + half * 512:
                                                    tt * 1024 + (half + 1) * 512]
                                    if shared:
                                        vector.tensor_tensor(dst, dst, po[t].ap(),
                                                             OP.add)
                                    else:
                                        vector.scalar_tensor_tensor(
                                            dst, po[t].ap(),
                                            combine.ap()[:, tt * 8 + e:
                                                         tt * 8 + e + 1],
                                            dst, OP.mult, OP.add)



def emit_gather_fm(p, xhat_tm, pfx):
    with p.scope():
        xfm = p.sbuf(pfx + "xfm", [128, ND * TPC], BF16)
        emit_tm_to_fm(p, xhat_tm, xfm)
        Gx = emit_allgather(p, pfx + "gx",
                            xfm.ap().rearrange("p (c t) -> p c t", c=ND),
                            [1024, TPC], BF16)
    return Gx


def emit_attn_gather_x(p, xhat_src, pfx, gain_bias=None, eps=1e-5):
    with p.scope():
        xfm = p.sbuf(pfx + "xfm", [128, ND * TPC], BF16)
        with p.scope():
            xhat = p.sbuf(pfx + "xhat", [128, NT * 1024], BF16)
            emit_pre_ln(p, xhat, eps=eps, src=xhat_src, gain_bias=gain_bias)
            emit_tm_to_fm(p, xhat, xfm)
        Gx = emit_allgather(p, pfx + "gx",
                            xfm.ap().rearrange("p (c t) -> p c t", c=ND),
                            [1024, TPC], BF16)
    return Gx


def emit_bcast_row(p, src_d, dst_sb, pfx, width=1024):
    nc = p.nc
    ld = p.sem(pfx + "bl")
    with p.scope():
        row = p.sbuf(pfx + "row", [1, width], BF16)
        pb = p.psum(pfx + "pb", [128, 512])
        with p.block() as blk:
            @blk.gpsimd
            def _(gpsimd):
                gpsimd.dma_start(out=row.ap(), in_=src_d.ap()).then_inc(ld.h, 16)
                ld.n += 16
            @blk.vector
            def _(vector):
                vector.wait_ge(ld.h, ld.n)
        for c in range(width // 512):
            with p.block() as blk:
                @blk.tensor
                def _(tensor, c=c):
                    tensor.matmul(pb.ap(), p.sb["ones_one"].ap(),
                                  row.ap()[0:1, c * 512:(c + 1) * 512],
                                  start=True, stop=True)
            with p.block() as blk:
                @blk.vector
                def _(vector, c=c):
                    vector.tensor_copy(dst_sb.ap()[:, c * 512:(c + 1) * 512],
                                       pb.ap())


def emit_attn_qkv(p, Gx, wqkv_d, bias_sb, Qd, Kd, Vd, KR, nq, nkv, q_scale,
                  pfx, rope=None, post_hook=None):
    nc = p.nc
    NFq = nq * 64 // 128
    NFk = nkv * 64 // 128 if nkv >= 2 else 1
    kwide = nkv * 64
    NF = NFq + 2 * NFk if nkv >= 2 else NFq + 2
    ident = p.sb["ident"]
    ld = p.sem(pfx + "ld")
    te = p.sem(pfx + "te")
    ev = p.sem(pfx + "ev")
    st = p.sem(pfx + "st")
    with p.scope():
        wsb = p.sbuf(pfx + "wsb", [128, ND * NF * 128], BF16)
        rowc = p.sbuf(pfx + "rowc", [1, S], BF16)
        with p.block() as blk:
            @blk.gpsimd
            def _(gpsimd):
                gpsimd.dma_start(
                    out=wsb.ap().rearrange("p (c k) -> p c k", c=ND),
                    in_=wqkv_d.ap().rearrange("(c p) k -> p c k", p=128)
                ).then_inc(ld.h, 16)
                ld.n += 16
                gpsimd.memset(rowc.ap(), -CSHIFT)
        with p.block() as blk:
            @blk.sync
            def _(sync):
                for h in range(nq):
                    sync.dma_start(out=Qd.ap()[h, KR - 1:KR, :], in_=rowc.ap()
                                   ).then_inc(st.h, 16)
                    st.n += 16
        with p.block() as blk:
            @blk.gpsimd
            def _(gpsimd):
                gpsimd.memset(rowc.ap(), 1.0)
        with p.block() as blk:
            @blk.sync
            def _(sync):
                for h in range(nkv):
                    sync.dma_start(out=Kd.ap()[h, KR - 1:KR, :], in_=rowc.ap()
                                   ).then_inc(st.h, 16)
                    st.n += 16
            @blk.vector
            def _(vector):
                vector.wait_ge(ld.h, ld.n)
        stage = p.sbuf(pfx + "stage", [128, NF * 512], BF16)
        stage2 = p.sbuf(pfx + "stage2", [128, NF * 512], BF16) if rope is not None else stage
        vtm = p.sbuf(pfx + "vtm", [128, NFk * 512], BF16)
        tstage = p.sbuf(pfx + "tst", [128, NFk * 4 * 128], BF16)
        xstr = p.sbuf(pfx + "xstr", [128, ND * 512], BF16)
        for r in range(4):
            for ts in range(TPC // 512):
                tok0 = r * TPC + ts * 512
                ld0, te0 = ld.n, te.n
                ld.n += 16 * ND
                te.n += 1
                with p.scope():
                  with p.scope():
                    ps = [p.psum(pfx + "ps", [128, 512]) for _ in range(NF)]
                    with p.block() as blk:
                        @blk.sync
                        def _(sync, r=r, ts=ts):
                            for dc in range(ND):
                                sync.dma_start(
                                    out=xstr.ap()[:, dc * 512:(dc + 1) * 512],
                                    in_=Gx.ap()[r * 1024 + dc * 128:
                                                r * 1024 + (dc + 1) * 128,
                                                ts * 512:(ts + 1) * 512]
                                ).then_inc(ld.h, 16)

                        @blk.tensor
                        def _(tensor):
                            for dc in range(ND):
                                tensor.wait_ge(ld.h, ld0 + 16 * (dc + 1))
                                for f in range(NF):
                                    inst = tensor.matmul(
                                        ps[f].ap(),
                                        wsb.ap()[:, dc * NF * 128 + f * 128:
                                                 dc * NF * 128 + (f + 1) * 128],
                                        xstr.ap()[:, dc * 512:(dc + 1) * 512],
                                        start=(dc == 0), stop=(dc == ND - 1))
                            inst.then_inc(te.h, 1)
                    with p.block() as blk:
                        @blk.vector
                        def _(vector, stw=st.n):
                            vector.wait_ge(st.h, stw)
                            for f in range(NF):
                                is_q = f < NFq
                                is_v = f >= NF - NFk
                                sc = q_scale if is_q else None
                                dst = (vtm.ap()[:, (f - (NF - NFk)) * 512:
                                                (f - (NF - NFk) + 1) * 512]
                                       if is_v else
                                       stage.ap()[:, f * 512:(f + 1) * 512])
                                if bias_sb is not None:
                                    vector.tensor_scalar(
                                        dst, ps[f].ap(),
                                        bias_sb.ap()[:, f:f + 1], sc,
                                        OP.add,
                                        OP.mult if sc is not None else OP.bypass)
                                elif sc is not None:
                                    vector.tensor_scalar(dst, ps[f].ap(), sc,
                                                         None, OP.mult)
                                else:
                                    vector.tensor_copy(dst, ps[f].ap())
                  if rope is not None:
                      emit_rope(p, stage, stage2, NFq, NFk, rope, tok0, pfx)
                  if post_hook is not None:
                      post_hook(stage, tok0)
                  pairs = []
                  for f in range(NFk):
                      for j in range(4):
                          pairs.append((
                              vtm.ap()[:, f * 512 + j * 128:
                                       f * 512 + (j + 1) * 128],
                              tstage.ap()[:, (f * 4 + j) * 128:
                                          (f * 4 + j + 1) * 128]))
                  emit_transpose_many(p, pairs)
                  with p.block() as blk:
                        @blk.sync
                        def _(sync, tok0=tok0):
                            for f in range(NFq):
                                for hh in range(2):
                                    if rope is None:
                                        sync.dma_start(
                                            out=Qd.ap()[2 * f + hh, 0:64,
                                                        tok0:tok0 + 512],
                                            in_=stage.ap()[hh * 64:hh * 64 + 64,
                                                           f * 512:(f + 1) * 512]
                                        ).then_inc(st.h, 16)
                                        st.n += 16
                                    else:
                                        sync.dma_start(
                                            out=Qd.ap()[2 * f + hh, 0:32,
                                                        tok0:tok0 + 512],
                                            in_=stage2.ap()[hh * 32:hh * 32 + 32,
                                                            f * 512:(f + 1) * 512]
                                        ).then_inc(st.h, 16)
                                        sync.dma_start(
                                            out=Qd.ap()[2 * f + hh, 32:64,
                                                        tok0:tok0 + 512],
                                            in_=stage2.ap()[64 + hh * 32:
                                                            64 + hh * 32 + 32,
                                                            f * 512:(f + 1) * 512]
                                        ).then_inc(st.h, 16)
                                        st.n += 32
                            for f in range(NFk):
                                nh = 2 if nkv >= 2 else 1
                                src_k = stage2 if rope is not None else stage
                                for hh in range(nh):
                                    if rope is None:
                                        sync.dma_start(
                                            out=Kd.ap()[nh * f + hh, 0:64,
                                                        tok0:tok0 + 512],
                                            in_=src_k.ap()[hh * 64:hh * 64 + 64,
                                                           (NFq + f) * 512:
                                                           (NFq + f + 1) * 512]
                                        ).then_inc(st.h, 16)
                                        st.n += 16
                                    else:
                                        sync.dma_start(
                                            out=Kd.ap()[nh * f + hh, 0:32,
                                                        tok0:tok0 + 512],
                                            in_=src_k.ap()[hh * 32:hh * 32 + 32,
                                                           (NFq + f) * 512:
                                                           (NFq + f + 1) * 512]
                                        ).then_inc(st.h, 16)
                                        sync.dma_start(
                                            out=Kd.ap()[nh * f + hh, 32:64,
                                                        tok0:tok0 + 512],
                                            in_=src_k.ap()[64 + hh * 32:
                                                           64 + hh * 32 + 32,
                                                           (NFq + f) * 512:
                                                           (NFq + f + 1) * 512]
                                        ).then_inc(st.h, 16)
                                        st.n += 32
                            for f in range(NFk):
                                for j in range(4):
                                    base = (f * 4 + j) * 128
                                    if nkv >= 2:
                                        sync.dma_start(
                                            out=Vd.ap()[2 * f, tok0 + j * 128:
                                                        tok0 + (j + 1) * 128, :],
                                            in_=tstage.ap()[:, base:base + 64]
                                        ).then_inc(st.h, 16)
                                        sync.dma_start(
                                            out=Vd.ap()[2 * f + 1,
                                                        tok0 + j * 128:
                                                        tok0 + (j + 1) * 128, :],
                                            in_=tstage.ap()[:, base + 64:base + 128]
                                        ).then_inc(st.h, 16)
                                        st.n += 32
                                    else:
                                        sync.dma_start(
                                            out=Vd.ap()[0, tok0 + j * 128:
                                                        tok0 + (j + 1) * 128, :],
                                            in_=tstage.ap()[:, base:base + 64]
                                        ).then_inc(st.h, 16)
                                        st.n += 16
        with p.block() as blk:
            @blk.sync
            def _(sync):
                sync.wait_ge(st.h, st.n)


def emit_rope(p, stage, stage2, NFq, NFk, rope_sb, tok0, pfx):
    fr = p.sem(pfx + "rp")
    tiles = [(f, 64) for f in range(NFq + NFk)]
    with p.scope():
        ta = p.sbuf(pfx + "rpa", [128, 512], F32)
        tb = p.sbuf(pfx + "rpb", [128, 512], F32)
        lo0 = p.sbuf(pfx + "rpl", [128, 512], BF16)
        r2 = p.sbuf(pfx + "rpr", [128, 512], BF16)
        with p.block() as blk:
            @blk.vector
            def _(vector):
                for f, hw in tiles:
                    sl = stage.ap()[:, f * 512:(f + 1) * 512]
                    d2 = stage2.ap()[:, f * 512:(f + 1) * 512]
                    cs = rope_sb[0].ap()[0:hw, tok0:tok0 + 512]
                    sn = rope_sb[1].ap()[0:hw, tok0:tok0 + 512]
                    up = sl[0:hw, :]
                    inst = vector.tensor_copy(lo0.ap()[0:hw, :], sl[hw:2 * hw, :])
                    fr.inc(inst, 1)
                    vector.wait_ge(fr.h, fr.n)
                    vector.tensor_tensor(ta.ap()[0:hw, :], up, cs, OP.mult)
                    inst = vector.tensor_tensor(tb.ap()[0:hw, :],
                                                lo0.ap()[0:hw, :], sn, OP.mult)
                    fr.inc(inst, 1)
                    vector.wait_ge(fr.h, fr.n)
                    inst = vector.tensor_tensor(d2[0:hw, :], ta.ap()[0:hw, :],
                                                tb.ap()[0:hw, :], OP.subtract)
                    fr.inc(inst, 1)
                    vector.wait_ge(fr.h, fr.n)
                    vector.tensor_tensor(ta.ap()[0:hw, :], up, sn, OP.mult)
                    inst = vector.tensor_tensor(tb.ap()[0:hw, :],
                                                lo0.ap()[0:hw, :], cs, OP.mult)
                    fr.inc(inst, 1)
                    vector.wait_ge(fr.h, fr.n)
                    inst = vector.tensor_tensor(r2.ap()[0:hw, :], ta.ap()[0:hw, :],
                                                tb.ap()[0:hw, :], OP.add)
                    fr.inc(inst, 1)
                    vector.wait_ge(fr.h, fr.n)
                    inst = vector.tensor_copy(d2[hw:2 * hw, :], r2.ap()[0:hw, :])
                    fr.inc(inst, 1)
                    vector.wait_ge(fr.h, fr.n)


def emit_attn_inner(p, Qd, Kd, Vd, CTXd, KR, nq, nkv, pfx, causal=False,
                    mask_sb=None, kvbufs=2):
    nc = p.nc
    ld = p.sem(pfx + "ld")
    ss = p.sem(pfx + "ss")
    tc = p.sem(pfx + "tc")
    se = p.sem(pfx + "se")
    ve = p.sem(pfx + "ve")
    st = p.sem(pfx + "st")
    NKT = S // 128
    NQS = S // 512
    kvs = []
    headkv = []
    prev = -1
    for h in range(nq):
        kvh = h if nkv == nq else h * nkv // nq
        if kvh != prev:
            kvs.append(kvh)
            prev = kvh
        headkv.append(len(kvs) - 1)
    nbuf = min(kvbufs, len(kvs))
    its = []
    for h in range(nq):
        for qs in range(NQS):
            nkt = min(NKT, 4 * qs + 4) if causal else NKT
            nmask = max(0, (min(nkt, 4 * qs + 4) - 4 * qs)) if causal else 0
            its.append(dict(h=h, qs=qs, nkt=nkt, nmask=nmask,
                            ngr=nkt // 2, kvi=headkv[h]))
    ld_base, ss_base, se_base, tc_base, ve_base, st_base = (
        ld.n, ss.n, se.n, tc.n, ve.n, st.n)
    kv_attach = {}
    head_first_m = [NQS * h for h in range(nq)]
    prev_at = -1
    for i in range(len(kvs)):
        m_need = head_first_m[next(h for h in range(nq) if headkv[h] == i)]
        at = max(prev_at + 1, 0,
                 m_need - (8 if nbuf > 1 else 0))
        kv_attach[at] = i
        prev_at = at
    ldr, ssr, ser, tcr, ver, str_ = (ld_base, ss_base, se_base, tc_base,
                                     ve_base, st_base)
    gg = 0
    for m, it in enumerate(its):
        if m in kv_attach:
            ldr += 32
        it["q_done"] = ldr + 16
        ldr += 16
        it["ss0"] = ssr
        ssr += it["nkt"]
        it["gg0"] = gg
        gg += it["ngr"]
        it["tc0"] = tcr
        tcr += 2
        it["ve0"] = ver
        ver += it["nmask"] + 3
        it["st0"] = str_
        str_ += 16
    tot_groups = gg
    head_ss_end = [its[NQS * (h + 1) - 1]["ss0"] + its[NQS * (h + 1) - 1]["nkt"]
                   for h in range(nq)]
    head_tc_end = [its[NQS * (h + 1) - 1]["tc0"] + 2 for h in range(nq)]
    kv_war = {}
    for i in range(len(kvs)):
        if i >= nbuf:
            hs = [h for h in range(nq) if headkv[h] == i - nbuf]
            kv_war[i] = (head_ss_end[max(hs)], head_tc_end[max(hs)])
    gg_of = {}
    for m, it in enumerate(its):
        for g in range(it["ngr"]):
            gg_of[it["gg0"] + g] = (m, g)
    ld.n, ss.n, se.n, tc.n, ve.n, st.n = ldr, ssr, se_base + tot_groups, tcr, ver, str_
    with p.scope():
        ksb = [p.sbuf(pfx + "ksb", [128, S], BF16) for _ in range(nbuf)]
        vsb = [p.sbuf(pfx + "vsb", [128, NKT * 65], BF16) for _ in range(nbuf)]
        qsb = [p.sbuf(pfx + "qsb", [128, 512], BF16) for _ in range(2)]
        pts = [p.sbuf(pfx + "pts", [128, 1024], BF16) for _ in range(2)]
        ctxs = p.sbuf(pfx + "ctxs", [128, 512], F32)
        rl = p.sbuf(pfx + "rl", [1, 512], BF16)
        ctxo = [p.sbuf(pfx + "ctxo", [128, 512], BF16) for _ in range(2)]
        pst = [p.psum(pfx + "pst", [128, 1024]) for _ in range(2)]
        pctx = p.psum(pfx + "pctx", [128, 512])
        prl = p.psum(pfx + "prl", [128, 512])
        with p.block() as blk:
            @blk.gpsimd
            def _(gpsimd):
                for kb in range(nbuf):
                    gpsimd.memset(
                        vsb[kb].ap().rearrange("p (c k) -> p c k", k=65)
                        [:, :, 64:65], 1.0)
        with p.block() as blk:
            @blk.sync
            def _(sync):
                i = kv_attach[0]
                kb = i % nbuf
                sync.dma_start(out=ksb[kb].ap()[0:KR, :],
                               in_=Kd.ap()[kvs[i]]).then_inc(ld.h, 16)
                sync.dma_start(
                    out=vsb[kb].ap().rearrange("p (c k) -> p c k",
                                               k=65)[:, :, 0:64],
                    in_=Vd.ap()[kvs[i]].rearrange("(c p) k -> p c k",
                                                  p=128)
                ).then_inc(ld.h, 16)
                sync.dma_start(
                    out=qsb[0].ap()[0:KR, :],
                    in_=Qd.ap()[its[0]["h"], :, its[0]["qs"] * 512:
                                (its[0]["qs"] + 1) * 512]
                ).then_inc(ld.h, 16)
                sync.wait_ge(ld.h, its[0]["q_done"])
        with p.block() as blk:
            @blk.sync
            def _(sync):
                for m, it in enumerate(its):
                    if m == 0:
                        continue
                    if m in kv_attach:
                        i = kv_attach[m]
                        if i in kv_war:
                            sync.wait_ge(ss.h, kv_war[i][0])
                            sync.wait_ge(tc.h, kv_war[i][1])
                        kb = i % nbuf
                        sync.dma_start(out=ksb[kb].ap()[0:KR, :],
                                       in_=Kd.ap()[kvs[i]]).then_inc(ld.h, 16)
                        sync.dma_start(
                            out=vsb[kb].ap().rearrange("p (c k) -> p c k",
                                                       k=65)[:, :, 0:64],
                            in_=Vd.ap()[kvs[i]].rearrange("(c p) k -> p c k",
                                                          p=128)
                        ).then_inc(ld.h, 16)
                    if m >= 2:
                        pit = its[m - 2]
                        sync.wait_ge(ss.h, pit["ss0"] + pit["nkt"])
                    sync.dma_start(
                        out=qsb[m % 2].ap()[0:KR, :],
                        in_=Qd.ap()[it["h"], :,
                                    it["qs"] * 512:(it["qs"] + 1) * 512]
                    ).then_inc(ld.h, 16)

            @blk.tensor
            def _(tensor):
                def scores(gg):
                    m, g = gg_of[gg]
                    it = its[m]
                    if g == 0:
                        tensor.wait_ge(ld.h, it["q_done"])
                    if gg >= 2:
                        tensor.wait_ge(se.h, se_base + gg - 1)
                    b = gg % 2
                    kb = it["kvi"] % nbuf
                    for j in range(2):
                        kt = 2 * g + j
                        tensor.matmul(
                            pst[b].ap()[:, j * 512:(j + 1) * 512],
                            ksb[kb].ap()[0:KR, kt * 128:(kt + 1) * 128],
                            qsb[m % 2].ap()[0:KR, :],
                            start=True, stop=True).then_inc(ss.h, 1)

                def ctx(gg):
                    m, g = gg_of[gg]
                    it = its[m]
                    tensor.wait_ge(se.h, se_base + gg + 1)
                    if g == 0 and m >= 1:
                        pit = its[m - 1]
                        tensor.wait_ge(ve.h, pit["ve0"] + pit["nmask"] + 2)
                    b = gg % 2
                    kb = it["kvi"] % nbuf
                    for j in range(2):
                        kt = 2 * g + j
                        inst = tensor.matmul(
                            pctx.ap()[0:65, :],
                            vsb[kb].ap()[:, kt * 65:(kt + 1) * 65],
                            pts[b].ap()[:, j * 512:(j + 1) * 512],
                            start=(kt == 0), stop=(kt == it["nkt"] - 1))
                    if 2 * g + 1 == it["nkt"] - 1:
                        inst.then_inc(tc.h, 1)

                def prlop(m):
                    it = its[m]
                    tensor.wait_ge(ve.h, it["ve0"] + it["nmask"] + 1)
                    tensor.matmul(prl.ap()[0:64, :],
                                  p.sb["ones_one"].ap()[0:1, 0:64],
                                  rl.ap(), start=True, stop=True
                                  ).then_inc(tc.h, 1)

                for gg in range(tot_groups):
                    scores(gg)
                    if gg >= 1:
                        ctx(gg - 1)
                        mprev, gprev = gg_of[gg - 1]
                        if gprev == its[mprev]["ngr"] - 1:
                            prlop(mprev)
                ctx(tot_groups - 1)
                prlop(len(its) - 1)

            @blk.scalar
            def _(scalar):
                for m, it in enumerate(its):
                    for g in range(it["ngr"]):
                        b = (it["gg0"] + g) % 2
                        scalar.wait_ge(ss.h, it["ss0"] + 2 * g + 2)
                        if causal:
                            nm_done = max(0, min(it["nmask"],
                                                 2 * g + 2 - 4 * it["qs"]))
                            if nm_done > 0:
                                scalar.wait_ge(ve.h, it["ve0"] + nm_done)
                        scalar.activation(pts[b].ap(), pst[b].ap(), AF.Exp
                                          ).then_inc(se.h, 1)

            @blk.vector
            def _(vector):
                for m, it in enumerate(its):
                    if causal:
                        for i in range(it["nmask"]):
                            kt = 4 * it["qs"] + i
                            ggk = it["gg0"] + kt // 2
                            b = ggk % 2
                            j = kt % 2
                            vector.wait_ge(ss.h, it["ss0"] + kt + 1)
                            vector.tensor_tensor(
                                pst[b].ap()[:, j * 512:(j + 1) * 512],
                                pst[b].ap()[:, j * 512:(j + 1) * 512],
                                mask_sb.ap()[:, i * 512:(i + 1) * 512],
                                OP.add).then_inc(ve.h, 1)
                    vector.wait_ge(tc.h, it["tc0"] + 1)
                    with p.nc.allow_low_precision(
                            reason="bf16 softmax normalizer"):
                        vector.reciprocal(rl.ap(), pctx.ap()[64:65, :]
                                          ).then_inc(ve.h, 1)
                    vector.tensor_copy(ctxs.ap()[0:64, :], pctx.ap()[0:64, :]
                                       ).then_inc(ve.h, 1)
                    vector.wait_ge(tc.h, it["tc0"] + 2)
                    if m >= 2:
                        vector.wait_ge(st.h, its[m - 2]["st0"] + 16)
                    vector.tensor_tensor(ctxo[m % 2].ap()[0:64, :],
                                         ctxs.ap()[0:64, :],
                                         prl.ap()[0:64, :], OP.mult
                                         ).then_inc(ve.h, 1)

            @blk.gpsimd
            def _(gpsimd):
                for m, it in enumerate(its):
                    gpsimd.wait_ge(ve.h, it["ve0"] + it["nmask"] + 3)
                    gpsimd.dma_start(
                        out=CTXd.ap()[it["h"] * 64:(it["h"] + 1) * 64,
                                      it["qs"] * 512:(it["qs"] + 1) * 512],
                        in_=ctxo[m % 2].ap()[0:64, :]).then_inc(st.h, 16)
        with p.block() as blk:
            @blk.sync
            def _(sync):
                sync.wait_ge(st.h, st.n)


def emit_attn_out(p, CTXg, wo_d, bias_d, pfx, nrows=1024, dst=None):
    nc = p.nc
    xres = dst if dst is not None else p.sb["xres"]
    ld = p.sem(pfx + "ld")
    ldc = p.sem(pfx + "ldc")
    te = p.sem(pfx + "te")
    NR = nrows // 128
    with p.scope():
        wob = [p.sbuf(pfx + "wob", [128, 1024], BF16) for _ in range(2)]
        csb = p.sbuf(pfx + "csb", [128, NR * 1024], BF16)
        bsb = p.sbuf(pfx + "bsb", [1, 1024], BF16)
        if bias_d is not None:
            with p.block() as blk:
                @blk.gpsimd
                def _(gpsimd):
                    gpsimd.dma_start(out=bsb.ap(), in_=bias_d.ap()
                                     ).then_inc(ld.h, 16)
                    ld.n += 16
        for half in range(2):
            with p.scope():
                po = [p.psum(pfx + "po", [128, 1024]) for _ in range(4)]
                ldc0 = ldc.n
                tewar = te.n
                with p.block() as blk:
                    @blk.sync
                    def _(sync, half=half, tewar=tewar):
                        choff = p.sync_reg(
                            ("attn_choff", half),
                            lambda: p.sync_reg(
                                "attn_col0", lambda: p.pid4() * 2048)
                            + half * 1024)
                        if half > 0:
                            sync.wait_ge(te.h, tewar)
                        sync.dma_start(
                            out=csb.ap().rearrange("p (c k) -> p c k", c=NR),
                            in_=CTXg.ap()
                            .rearrange("(c p) s -> p c s", p=128)
                            [:, :, bass.ds(choff, 1024)]
                        ).then_inc(ldc.h, 16)
                        ldc.n += 16
                for qtr in range(2):
                    tb0 = half * 8 + qtr * 4
                    ld0, te0 = ld.n, te.n
                    te.n += NR
                    with p.block() as blk:
                        @blk.gpsimd
                        def _(gpsimd, tb0=tb0):
                            for rc in range(NR):
                                b = rc % 2
                                if rc >= 2:
                                    gpsimd.wait_ge(te.h, te0 + rc - 1)
                                gpsimd.dma_start(
                                    out=wob[b].ap(),
                                    in_=wo_d.ap()[rc * 128:(rc + 1) * 128, :]
                                ).then_inc(ld.h, 16)
                                ld.n += 16

                        @blk.tensor
                        def _(tensor, qtr=qtr):
                            tensor.wait_ge(ldc.h, ldc0 + 16)
                            for rc in range(NR):
                                b = rc % 2
                                tensor.wait_ge(ld.h, ld0 + 16 * (rc + 1))
                                last = (rc == NR - 1) and bias_d is None
                                for t in range(4):
                                    for hf in range(2):
                                        inst = tensor.matmul(
                                            po[t].ap()[:, hf * 512:(hf + 1) * 512],
                                            csb.ap()[:, rc * 1024 + qtr * 512
                                                     + t * 128:
                                                     rc * 1024 + qtr * 512
                                                     + (t + 1) * 128],
                                            wob[b].ap()[:, hf * 512:(hf + 1) * 512],
                                            start=(rc == 0), stop=last)
                                inst.then_inc(te.h, 1)
                            if bias_d is not None:
                                for t in range(4):
                                    for hf in range(2):
                                        tensor.matmul(
                                            po[t].ap()[:, hf * 512:(hf + 1) * 512],
                                            p.sb["ones_one"].ap(),
                                            bsb.ap()[0:1, hf * 512:(hf + 1) * 512],
                                            start=False, stop=True)
                    tok = te.n
                    with p.block() as blk:
                        @blk.vector
                        def _(vector, tb0=tb0):
                            for t in range(4):
                                tt = tb0 + t
                                vector.tensor_tensor(
                                    xres.ap()[:, tt * 1024:(tt + 1) * 1024],
                                    xres.ap()[:, tt * 1024:(tt + 1) * 1024],
                                    po[t].ap(), OP.add)


def prep_L1_attn(core, inputs):
    h0 = (core % 4) * 4
    aw = inputs["L1_attn_attn_w"]
    ab = inputs["L1_attn_attn_b"]
    awT = aw.T
    cols = np.concatenate([
        np.arange(h0 * 64, (h0 + 4) * 64),
        1024 + np.arange(h0 * 64, (h0 + 4) * 64),
        2048 + np.arange(h0 * 64, (h0 + 4) * 64)])
    bias = ab[cols]
    return {
        "l1a_wqkv": np.ascontiguousarray(awT[:, cols]),
        "l1a_bias": np.ascontiguousarray(bias.reshape(6, 128).T),
        "l1a_woT": np.ascontiguousarray(inputs["L1_attn_proj_w"].T),
        "l1a_pb": np.ascontiguousarray(inputs["L1_attn_proj_b"][None, :]),
    }


def emit_L1_attn(p):
    nc = p.nc
    wqkv = p.dram("l1a_wqkv", [D, 768], F32, kind="ExternalInput")
    bias_d = p.dram("l1a_bias", [128, 6], F32, kind="ExternalInput")
    woT = p.dram("l1a_woT", [D, D], F32, kind="ExternalInput")
    pb = p.dram("l1a_pb", [1, D], F32, kind="ExternalInput")
    KR = 65
    Qd = p.dram("l1a_Q", [4, KR, S], BF16)
    Kd = p.dram("l1a_K", [4, KR, S], BF16)
    Vd = p.dram("l1a_V", [4, S, 64], BF16)
    with p.scope():
        bias_sb = p.sbuf("l1a_bsb", [128, 6], F32)
        mask_sb = p.sbuf("l1a_mask", [128, 4 * 512], F32)
        ld = p.sem("l1a_ld0")
        with p.block() as blk:
            @blk.sync
            def _(sync):
                sync.dma_start(out=bias_sb.ap(), in_=bias_d.ap()
                               ).then_inc(ld.h, 16)
                ld.n += 16
            @blk.gpsimd
            def _(gpsimd):
                gpsimd.iota(mask_sb.ap().rearrange("p (o q) -> p o q", o=4),
                            [[0, 4], [1, 512]], channel_multiplier=0,
                            allow_small_or_imprecise_dtypes=True)
            @blk.vector
            def _(vector):
                vector.wait_ge(ld.h, ld.n)
        with p.block() as blk:
            @blk.vector
            def _(vector):
                for o in range(4):
                    vector.tensor_scalar(
                        mask_sb.ap()[:, o * 512:(o + 1) * 512],
                        mask_sb.ap()[:, o * 512:(o + 1) * 512],
                        p.sb["iota_p"].ap()[:, 0:1], float(-o * 128),
                        OP.subtract, OP.add)
        with p.block() as blk:
            @blk.vector
            def _(vector):
                vector.tensor_scalar(mask_sb.ap(), mask_sb.ap(),
                                     0.0, -1e30, OP.is_lt, OP.mult)
        Gx = emit_attn_gather_x(p, p.sb["xres"], "l1a_")
        emit_attn_qkv(p, Gx, wqkv, bias_sb, Qd, Kd, Vd, KR, 4, 4, 0.125,
                      "l1aq_")
        CTXd = p.dram("l1a_ctx", [256, S], BF16)
        emit_attn_inner(p, Qd, Kd, Vd, CTXd, KR, 4, 4, "l1ai_", causal=True,
                        mask_sb=mask_sb)
        CTXg = emit_allgather(p, "l1a_ctxg", CTXd.ap(), [256, S], BF16)
        emit_attn_out(p, CTXg, woT, pb, "l1ao_")


def prep_L2_attn(core, inputs):
    rank = core % 4
    h0 = rank * 4
    kv = rank
    wq = inputs["L2_attn_w_q"]
    wk = inputs["L2_attn_w_k"]
    wv = inputs["L2_attn_w_v"]
    qcols = []
    for f in range(2):
        a, b = h0 + 2 * f, h0 + 2 * f + 1
        qcols += list(range(a * 64, a * 64 + 32))
        qcols += list(range(b * 64, b * 64 + 32))
        qcols += list(range(a * 64 + 32, a * 64 + 64))
        qcols += list(range(b * 64 + 32, b * 64 + 64))
    slab = np.zeros((1024, 512), np.float32)
    slab[:, 0:256] = wq[:, qcols]
    kk = wk[:, kv * 64:(kv + 1) * 64]
    slab[:, 256:288] = kk[:, 0:32]
    slab[:, 288:320] = kk[:, 0:32]
    slab[:, 320:352] = kk[:, 32:64]
    slab[:, 352:384] = kk[:, 32:64]
    slab[:, 384:448] = wv[:, kv * 64:(kv + 1) * 64]
    half = 32
    inv_freq = 1.0 / (10000.0 ** (np.arange(0, half, dtype=np.float64) * 2.0 / 64))
    pos = np.arange(S, dtype=np.float64)
    fr = pos[:, None] * inv_freq[None, :]
    cosT = np.cos(fr).astype(np.float32).T
    sinT = np.sin(fr).astype(np.float32).T
    tab = np.concatenate([cosT, cosT, sinT, sinT], 0)
    return {
        "l2a_wqkv": np.ascontiguousarray(slab),
        "l2a_rope": np.ascontiguousarray(tab),
        "l2a_woT": np.ascontiguousarray(inputs["L2_attn_w_o"]),
    }


def emit_L2_attn(p):
    nc = p.nc
    wqkv = p.dram("l2a_wqkv", [D, 512], F32, kind="ExternalInput")
    rope_d = p.dram("l2a_rope", [128, S], F32, kind="ExternalInput")
    woT = p.dram("l2a_woT", [D, D], F32, kind="ExternalInput")
    KR = 65
    Qd = p.dram("l2a_Q", [4, KR, S], BF16)
    Kd = p.dram("l2a_K", [1, KR, S], BF16)
    Vd = p.dram("l2a_V", [1, S, 64], BF16)
    with p.scope():
        rope_cos = p.sbuf("l2a_ropec", [64, S], BF16)
        rope_sin = p.sbuf("l2a_ropes", [64, S], BF16)
        rope_sb = (rope_cos, rope_sin)
        ld = p.sem("l2a_ld0")
        with p.block() as blk:
            @blk.gpsimd
            def _(gpsimd):
                gpsimd.dma_start(out=rope_cos.ap(), in_=rope_d.ap()[0:64, :]
                                 ).then_inc(ld.h, 16)
                gpsimd.dma_start(out=rope_sin.ap(), in_=rope_d.ap()[64:128, :]
                                 ).then_inc(ld.h, 16)
                ld.n += 32
            @blk.vector
            def _(vector):
                vector.wait_ge(ld.h, ld.n)
        Gx = emit_attn_gather_x(p, p.sb["xres"], "l2a_")
        emit_attn_qkv(p, Gx, wqkv, None, Qd, Kd, Vd, KR, 4, 1, 0.125,
                      "l2aq_", rope=rope_sb)
        CTXd = p.dram("l2a_ctx", [256, S], BF16)
        emit_attn_inner(p, Qd, Kd, Vd, CTXd, KR, 4, 1, "l2ai_")
        CTXg = emit_allgather(p, "l2a_ctxg", CTXd.ap(), [256, S], BF16)
        emit_attn_out(p, CTXg, woT, None, "l2ao_")



def prep_L0_attn(core, inputs):
    rank = core % 4
    h0 = rank * 4
    Wq = inputs["L0_attn_Wq"].reshape(D, 1024)
    Wk = inputs["L0_attn_Wk"].reshape(D, 1024)
    Wv = inputs["L0_attn_Wv"].reshape(D, 1024)
    cols = np.arange(h0 * 64, (h0 + 4) * 64)
    slab = np.concatenate([Wq[:, cols], Wk[:, cols], Wv[:, cols]], 1)
    Whq = inputs["L0_attn_Whq"]
    Whk = inputs["L0_attn_Whk"]
    def packWh(W):
        out = np.zeros((128, 64), np.float32)
        out[0:64, 0:32] = W[h0 + 0]
        out[64:128, 0:32] = W[h0 + 1]
        out[0:64, 32:64] = W[h0 + 2]
        out[64:128, 32:64] = W[h0 + 3]
        return out
    return {
        "l0a_wqkv": np.ascontiguousarray(slab),
        "l0a_whq": packWh(Whq),
        "l0a_whk": packWh(Whk),
        "l0a_wo": np.ascontiguousarray(inputs["L0_attn_Wo"].reshape(1024, D)),
        "l0a_ln1g": np.ascontiguousarray(inputs["L0_attn_ln1_g"][None, :]),
        "l0a_ln1b": np.ascontiguousarray(inputs["L0_attn_ln1_b"][None, :]),
        "l0a_ln2g": np.ascontiguousarray(inputs["L0_attn_ln2_g"][None, :]),
        "l0a_ln2b": np.ascontiguousarray(inputs["L0_attn_ln2_b"][None, :]),
        "l0a_w1": np.ascontiguousarray(inputs["L0_attn_W1"]),
        "l0a_b1": np.ascontiguousarray(
            inputs["L0_attn_b1"].reshape(32, 128).T),
        "l0a_w2": np.ascontiguousarray(inputs["L0_attn_W2"]),
        "l0a_b2": np.ascontiguousarray(inputs["L0_attn_b2"][None, :]),
    }


def emit_qbkb_hook(p, Qd, Kd, whq_sb, whk_sb, pfx):
    st = p.sem(pfx + "st")
    fv = p.sem(pfx + "fv")
    fs = p.sem(pfx + "fs")
    te = p.sem(pfx + "te")

    def hook(stage, tok0):
        with p.scope():
            pqb = [p.psum(pfx + "pqb", [128, 32]) for _ in range(4)]
            ptr = p.psum(pfx + "ptr", [128, 128], BF16)
            qsm = p.sbuf(pfx + "qsm", [128, 128], F32)
            gs = p.sbuf(pfx + "gs", [128, 4], F32)
            qbb = p.sbuf(pfx + "qbb", [128, 128], BF16)
            qbT2 = [p.sbuf(pfx + "qbT", [128, 128], BF16) for _ in range(2)]
            for which, (Wsb, Dd, scale_out) in enumerate(
                    [(whq_sb, Qd, 0.1), (whk_sb, Kd, 1.0)]):
                foff = 0 if which == 0 else 2
                lscale = 8.0 if which == 0 else 1.0
                for sub in range(4):
                    t0 = tok0 + sub * 128
                    qbT = qbT2[sub % 2]
                    st_prev = st.n - 64
                    te0, fv0, fs0 = te.n, fv.n, fs.n
                    te.n += 2
                    fv.n += 9
                    fs.n += 1
                    with p.block() as blk:
                        @blk.tensor
                        def _(tensor, sub=sub, foff=foff, Wsb=Wsb):
                            for hh in range(4):
                                ft = foff + hh // 2
                                hb = (hh % 2) * 64
                                inst = tensor.matmul(
                                    pqb[hh].ap(),
                                    stage.ap()[hb:hb + 64,
                                               ft * 512 + sub * 128:
                                               ft * 512 + (sub + 1) * 128],
                                    Wsb.ap()[hb:hb + 64,
                                             (hh // 2) * 32:(hh // 2) * 32 + 32],
                                    start=True, stop=True)
                            inst.then_inc(te.h, 1)
                            tensor.wait_ge(fv.h, fv0 + 9)
                            tensor.transpose(ptr.ap(), qbb.ap(),
                                             p.sb["ident"].ap()).then_inc(te.h, 1)

                        @blk.vector
                        def _(vector, sub=sub, lscale=lscale, scale_out=scale_out):
                            vector.wait_ge(te.h, te0 + 1)
                            for hh in range(4):
                                inst = vector.tensor_copy(
                                    qsm.ap()[:, hh * 32:(hh + 1) * 32],
                                    pqb[hh].ap())
                            inst.then_inc(fv.h, 1)
                            vector.wait_ge(fv.h, fv0 + 1)
                            vector.wait_ge(fs.h, fs0 + 1)
                            grp = qsm.ap().rearrange("p (g k) -> p g k", k=32)
                            inst = vector.reduce_sum(gs.ap(), grp, axis=AX.X)
                            inst.then_inc(fv.h, 1)
                            vector.wait_ge(fv.h, fv0 + 2)
                            inst = vector.reciprocal(gs.ap(), gs.ap())
                            inst.then_inc(fv.h, 1)
                            vector.wait_ge(fv.h, fv0 + 3)
                            for g in range(4):
                                inst = vector.tensor_scalar(
                                    qsm.ap()[:, g * 32:(g + 1) * 32],
                                    qsm.ap()[:, g * 32:(g + 1) * 32],
                                    gs.ap()[:, g:g + 1], scale_out,
                                    OP.mult, OP.mult)
                                inst.then_inc(fv.h, 1)
                            vector.wait_ge(fv.h, fv0 + 7)
                            inst = vector.tensor_copy(qbb.ap(), qsm.ap())
                            inst.then_inc(fv.h, 1)
                            vector.wait_ge(fv.h, fv0 + 8)
                            inst = vector.engine_nop()
                            inst.then_inc(fv.h, 1)
                            vector.wait_ge(te.h, te0 + 2)
                            if st_prev >= 0:
                                vector.wait_ge(st.h, max(0, st_prev + 64))
                            inst = vector.tensor_copy(qbT.ap(), ptr.ap())

                        @blk.scalar
                        def _(scalar, lscale=lscale):
                            scalar.wait_ge(fv.h, fv0 + 1)
                            scalar.activation(qsm.ap(), qsm.ap(), AF.Exp,
                                              scale=lscale).then_inc(fs.h, 1)
                    with p.block() as blk:
                        @blk.sync
                        def _(sync, sub=sub, t0=t0, Dd=Dd):
                            for hh in range(4):
                                sync.dma_start(
                                    out=Dd.ap()[hh, 64:96, t0:t0 + 128],
                                    in_=qbT.ap()[hh * 32:(hh + 1) * 32, :]
                                ).then_inc(st.h, 16)
                                st.n += 16
    return hook


def emit_L0_attn(p):
    nc = p.nc
    wqkv = p.dram("l0a_wqkv", [D, 768], F32, kind="ExternalInput")
    whq_d = p.dram("l0a_whq", [128, 64], F32, kind="ExternalInput")
    whk_d = p.dram("l0a_whk", [128, 64], F32, kind="ExternalInput")
    wo_d = p.dram("l0a_wo", [1024, D], F32, kind="ExternalInput")
    ln1g = p.dram("l0a_ln1g", [1, D], F32, kind="ExternalInput")
    ln1b = p.dram("l0a_ln1b", [1, D], F32, kind="ExternalInput")
    ln2g = p.dram("l0a_ln2g", [1, D], F32, kind="ExternalInput")
    ln2b = p.dram("l0a_ln2b", [1, D], F32, kind="ExternalInput")
    w1_d = p.dram("l0a_w1", [D, 4096], F32, kind="ExternalInput")
    b1_d = p.dram("l0a_b1", [128, 32], F32, kind="ExternalInput")
    w2_d = p.dram("l0a_w2", [4096, D], F32, kind="ExternalInput")
    b2_d = p.dram("l0a_b2", [1, D], F32, kind="ExternalInput")
    KR = 97
    Qd = p.dram("l0a_Q", [4, KR, S], BF16)
    Kd = p.dram("l0a_K", [4, KR, S], BF16)
    Vd = p.dram("l0a_V", [4, S, 64], BF16)
    xres = p.sb["xres"]
    with p.scope():
        xh0 = p.sbuf("l0a_xh0", [128, NT * 1024], F32)
        emit_pre_ln(p, xh0, eps=1e-5)
        whq_sb = p.sbuf("l0a_whq", [128, 64], BF16)
        whk_sb = p.sbuf("l0a_whk", [128, 64], BF16)
        ld = p.sem("l0a_ld0")
        with p.block() as blk:
            @blk.gpsimd
            def _(gpsimd):
                gpsimd.dma_start(out=whq_sb.ap(), in_=whq_d.ap()
                                 ).then_inc(ld.h, 16)
                gpsimd.dma_start(out=whk_sb.ap(), in_=whk_d.ap()
                                 ).then_inc(ld.h, 16)
                ld.n += 32
            @blk.vector
            def _(vector):
                vector.wait_ge(ld.h, ld.n)
        with p.scope():
            gbc = p.sbuf("l0a_gbc", [128, 1024], F32)
            bbc = p.sbuf("l0a_bbc", [128, 1024], F32)
            emit_bcast_row(p, ln1g, gbc, "l0a_g1")
            emit_bcast_row(p, ln1b, bbc, "l0a_b1")
            x1 = p.sbuf("l0a_x1", [128, NT * 1024], BF16)
            emit_pre_ln(p, x1, eps=1e-6, src=xh0, gain_bias=(gbc, bbc))
            Gx = emit_gather_fm(p, x1, "l0a_")
        hook = emit_qbkb_hook(p, Qd, Kd, whq_sb, whk_sb, "l0ah_")
        emit_attn_qkv(p, Gx, wqkv, None, Qd, Kd, Vd, KR, 4, 4, 0.125,
                      "l0aq_", post_hook=hook)
        hst = p.sem("l0ah_st")
        with p.block() as blk:
            @blk.sync
            def _(sync):
                sync.wait_ge(hst.h, hst.n)
        CTXd = p.dram("l0a_ctx", [256, S], BF16)
        emit_attn_inner(p, Qd, Kd, Vd, CTXd, KR, 4, 4, "l0ai_")
        CTXg = emit_allgather(p, "l0a_ctxg", CTXd.ap(), [256, S], BF16)
        emit_attn_out(p, CTXg, wo_d, None, "l0ao_", dst=xh0)
        with p.scope():
            x2fm = p.sbuf("l0a_x2fm", [128, ND * TPC], BF16)
            with p.scope():
                g2bc = p.sbuf("l0a_g2bc", [128, 1024], F32)
                b2bc = p.sbuf("l0a_b2bc", [128, 1024], F32)
                emit_bcast_row(p, ln2g, g2bc, "l0a_g2")
                emit_bcast_row(p, ln2b, b2bc, "l0a_b2")
                x2n = p.sbuf("l0a_x2n", [128, NT * 1024], BF16)
                emit_pre_ln(p, x2n, eps=1e-6, src=xh0, gain_bias=(g2bc, b2bc))
                emit_tm_to_fm(p, x2n, x2fm)
            b1sb = p.sbuf("l0a_b1sb", [128, 32], F32)
            b2row = p.sbuf("l0a_b2row", [1, 1024], BF16)
            with p.block() as blk:
                @blk.gpsimd
                def _(gpsimd):
                    gpsimd.dma_start(out=b1sb.ap(), in_=b1_d.ap()
                                     ).then_inc(ld.h, 16)
                    gpsimd.dma_start(out=b2row.ap(), in_=b2_d.ap()
                                     ).then_inc(ld.h, 16)
                    ld.n += 32
                @blk.vector
                def _(vector):
                    vector.wait_ge(ld.h, ld.n)
            wbuf = [p.sbuf("l0a_wbuf", [128, 1024], BF16) for _ in range(2)]
            hbuf = p.sbuf("l0a_hbuf", [128, 32 * 512], BF16)
            ldm = p.sem("l0a_ldm")
            tem = p.sem("l0a_tem")
            sem = p.sem("l0a_sem")
            for tt in range(4):
                with p.scope():
                    ph = [p.psum("l0a_ph", [128, 512]) for _ in range(2)]
                    ld0, te0, se0 = ldm.n, tem.n, sem.n
                    ldm.n += 16 * 32
                    tem.n += 32
                    sem.n += 32
                    with p.block() as blk:
                        @blk.gpsimd
                        def _(gpsimd):
                            for hc in range(32):
                                if hc >= 2:
                                    gpsimd.wait_ge(tem.h, te0 + hc - 1)
                                gpsimd.dma_start(
                                    out=wbuf[hc % 2].ap()
                                    .rearrange("p (c k) -> p c k", c=ND),
                                    in_=w1_d.ap()[:, hc * 128:(hc + 1) * 128]
                                    .rearrange("(c p) k -> p c k", p=128)
                                ).then_inc(ldm.h, 16)

                        @blk.tensor
                        def _(tensor, tt=tt):
                            for hc in range(32):
                                b = hc % 2
                                tensor.wait_ge(ldm.h, ld0 + 16 * (hc + 1))
                                if hc >= 2:
                                    tensor.wait_ge(sem.h, se0 + hc - 1)
                                for dc in range(ND):
                                    inst = tensor.matmul(
                                        ph[b].ap(),
                                        wbuf[b].ap()[:, dc * 128:(dc + 1) * 128],
                                        x2fm.ap()[:, dc * TPC + tt * 512:
                                                  dc * TPC + (tt + 1) * 512],
                                        start=(dc == 0), stop=(dc == ND - 1))
                                inst.then_inc(tem.h, 1)

                        @blk.scalar
                        def _(scalar):
                            for hc in range(32):
                                b = hc % 2
                                scalar.wait_ge(tem.h, te0 + hc + 1)
                                scalar.activation(
                                    hbuf.ap()[:, hc * 512:(hc + 1) * 512],
                                    ph[b].ap(), AF.Relu,
                                    bias=b1sb.ap()[:, hc:hc + 1]
                                ).then_inc(sem.h, 1)
                with p.scope():
                    po = [p.psum("l0a_po", [128, 1024]) for _ in range(4)]
                    ld0, te0 = ldm.n, tem.n
                    ldm.n += 16 * 32
                    tem.n += 32
                    with p.block() as blk:
                        @blk.gpsimd
                        def _(gpsimd):
                            for hc in range(32):
                                if hc >= 2:
                                    gpsimd.wait_ge(tem.h, te0 + hc - 1)
                                gpsimd.dma_start(
                                    out=wbuf[hc % 2].ap(),
                                    in_=w2_d.ap()[hc * 128:(hc + 1) * 128, :]
                                ).then_inc(ldm.h, 16)

                        @blk.tensor
                        def _(tensor, tt=tt):
                            for t in range(4):
                                for hf in range(2):
                                    tensor.matmul(
                                        po[t].ap()[:, hf * 512:(hf + 1) * 512],
                                        p.sb["ones_one"].ap(),
                                        b2row.ap()[0:1, hf * 512:(hf + 1) * 512],
                                        start=True, stop=False)
                            for hc in range(32):
                                b = hc % 2
                                tensor.wait_ge(ldm.h, ld0 + 16 * (hc + 1))
                                for t in range(4):
                                    for hf in range(2):
                                        inst = tensor.matmul(
                                            po[t].ap()[:, hf * 512:(hf + 1) * 512],
                                            hbuf.ap()[:, hc * 512 + t * 128:
                                                      hc * 512 + (t + 1) * 128],
                                            wbuf[b].ap()[:, hf * 512:(hf + 1) * 512],
                                            start=False, stop=(hc == 31))
                                inst.then_inc(tem.h, 1)
                    with p.block() as blk:
                        @blk.vector
                        def _(vector, tt=tt):
                            fvv = p.sem("l0a_fv")
                            for t in range(4):
                                tn = tt * 4 + t
                                inst = vector.tensor_tensor(
                                    xres.ap()[:, tn * 1024:(tn + 1) * 1024],
                                    xres.ap()[:, tn * 1024:(tn + 1) * 1024],
                                    po[t].ap(), OP.add)
                            inst.then_inc(fvv.h, 1)
                            fvv.n += 1
                            vector.wait_ge(fvv.h, fvv.n)
                            for t in range(4):
                                tn = tt * 4 + t
                                vector.tensor_tensor(
                                    xres.ap()[:, tn * 1024:(tn + 1) * 1024],
                                    xres.ap()[:, tn * 1024:(tn + 1) * 1024],
                                    xh0.ap()[:, tn * 1024:(tn + 1) * 1024],
                                    OP.add)



def prep_L3_attn(core, inputs):
    rank = core % 4
    mixes = np.stack([inputs["L3_attn_x_" + n] for n in
                      ["r", "k", "v", "w", "a", "g"]], 1)
    mixes_pc = mixes.reshape(8, 128, 6).transpose(1, 0, 2).reshape(128, 48)
    def pc(v):
        return np.ascontiguousarray(v.reshape(8, 128).T)
    Wrkv = np.concatenate([inputs["L3_attn_W_r"], inputs["L3_attn_W_k"],
                           inputs["L3_attn_W_v"]], 1)
    return {
        "l3a_mix": np.ascontiguousarray(mixes_pc),
        "l3a_wrkv": Wrkv,
        "l3a_w1": np.ascontiguousarray(inputs["L3_attn_w1"]),
        "l3a_w2": np.ascontiguousarray(inputs["L3_attn_w2"]),
        "l3a_a1": np.ascontiguousarray(inputs["L3_attn_a1"]),
        "l3a_a2": np.ascontiguousarray(inputs["L3_attn_a2"]),
        "l3a_g1": np.ascontiguousarray(inputs["L3_attn_g1"]),
        "l3a_g2": np.ascontiguousarray(inputs["L3_attn_g2"]),
        "l3a_w0": pc(inputs["L3_attn_w0"]),
        "l3a_a0": pc(inputs["L3_attn_a0"]),
        "l3a_kk": pc(inputs["L3_attn_k_k"]),
        "l3a_ka": pc(inputs["L3_attn_k_a"]),
        "l3a_rk": np.ascontiguousarray(
            inputs["L3_attn_r_k"].reshape(1, 1024)),
        "l3a_rkpc": np.ascontiguousarray(
            inputs["L3_attn_r_k"].reshape(1024).reshape(8, 128).T),
        "l3a_lnw": np.ascontiguousarray(inputs["L3_attn_ln_w"][None, :]),
        "l3a_lnb": np.ascontiguousarray(inputs["L3_attn_ln_b"][None, :]),
        "l3a_wo": np.ascontiguousarray(inputs["L3_attn_W_o"]),
        "l3a_bmask": np.full((128, 1), 0.0 if rank == 0 else 1.0, np.float32),
    }


def emit_l3_head_algebra(p, c, smat, tmm, efm, hm, mko, mki, id64,
                         Md, Cd, G1d, YLd, st, eL32=None):
    nc = p.nc
    te = p.sem("l3h_te")
    ev = p.sem("l3h_ev")
    dv = p.sem("l3h_dv")
    with p.scope():
        pp = [p.psum("l3h_pp", [64, 128]) for _ in range(4)]
        mko16, mki16, mklo16 = (p.sb["l3_mko16"], p.sb["l3_mki16"],
                                p.sb["l3_mklo16"])

        def head_ops(h):
            hp = h // 2
            hb = (h % 2) * 64
            psl = slice(hp * 64, (hp + 1) * 64)
            def fm(n):
                return smat[n].ap()[hb:hb + 64, psl]
            def tm(n):
                return tmm[n].ap()[:, hp * 128 + (h % 2) * 64:
                                   hp * 128 + (h % 2) * 64 + 64]
            return fm, tm

        prods = [("A", "wt", "ut"), ("AT", "ut", "wt"), ("B", "kt", "ut"),
                 ("Uw", "wt", "rt"), ("Uk", "kt", "rt")]
        with p.block() as blk:
            te0, ev0 = te.n, ev.n
            te.n += 5 * 16
            ev.n += 5 * 16

            @blk.tensor
            def _(tensor):
                i = 0
                for nm, a, b in prods:
                    for h in range(16):
                        fm, tm = head_ops(h)
                        if i >= 4:
                            tensor.wait_ge(ev.h, ev0 + i - 3)
                        tensor.matmul(pp[i % 4].ap()[:, 0:64], fm(a), fm(b),
                                      start=True, stop=True).then_inc(te.h, 1)
                        i += 1

            @blk.scalar
            def _(scalar):
                i = 0
                for nm, a, b in prods:
                    for h in range(16):
                        scalar.wait_ge(te.h, te0 + i + 1)
                        scalar.copy(hm[nm].ap()[:, h * 64:(h + 1) * 64],
                                    pp[i % 4].ap()[:, 0:64]).then_inc(ev.h, 1)
                        i += 1
        with p.block() as blk:
            @blk.vector
            def _(vector):
                vector.tensor_tensor(hm["A"].ap(), hm["A"].ap(), mko16.ap(),
                                     OP.mult)
                vector.tensor_tensor(hm["AT"].ap(), hm["AT"].ap(), mklo16.ap(),
                                     OP.mult)
                vector.tensor_tensor(hm["B"].ap(), hm["B"].ap(), mko16.ap(),
                                     OP.mult)
                vector.tensor_tensor(hm["Uw"].ap(), hm["Uw"].ap(), mki16.ap(),
                                     OP.mult)
                vector.tensor_tensor(hm["Uk"].ap(), hm["Uk"].ap(), mki16.ap(),
                                     OP.mult)
        sqs = [("A2", "AT", "A"), ("A2T", "A", "AT"), ("A4", "A2T", "A2"),
               ("A4T", "A2", "A2T"), ("A8", "A4T", "A4"), ("A8T", "A4", "A4T"),
               ("A16", "A8T", "A8"), ("A16T", "A8", "A8T"),
               ("A32", "A16T", "A16")]
        for gi in range(0, len(sqs), 2):
            grp = sqs[gi:gi + 2]
            with p.block() as blk:
                te0, ev0 = te.n, ev.n
                nun = len(grp) * 16
                te.n += nun
                ev.n += nun

                @blk.tensor
                def _(tensor, grp=grp):
                    i = 0
                    for nm, a, b in grp:
                        for h in range(16):
                            sl = slice(h * 64, (h + 1) * 64)
                            if i >= 4:
                                tensor.wait_ge(ev.h, ev0 + i - 3)
                            tensor.matmul(pp[i % 4].ap()[:, 0:64],
                                          hm[a].ap()[:, sl], hm[b].ap()[:, sl],
                                          start=True, stop=True).then_inc(te.h, 1)
                            i += 1

                @blk.scalar
                def _(scalar, grp=grp):
                    i = 0
                    for nm, a, b in grp:
                        for h in range(16):
                            scalar.wait_ge(te.h, te0 + i + 1)
                            scalar.copy(hm[nm].ap()[:, h * 64:(h + 1) * 64],
                                        pp[i % 4].ap()[:, 0:64]).then_inc(ev.h, 1)
                            i += 1
        with p.block() as blk:
            te0, ev0 = te.n, ev.n
            te.n += 16
            ev.n += 2 * 16

            @blk.tensor
            def _(tensor):
                for h in range(16):
                    fm, tm = head_ops(h)
                    if h >= 4:
                        tensor.wait_ge(ev.h, ev0 + 2 * (h - 3))
                    tensor.matmul(pp[h % 4].ap()[:, 0:64],
                                  hm["B"].ap()[:, h * 64:(h + 1) * 64],
                                  tm("v"), start=True, stop=True
                                  ).then_inc(te.h, 1)

            @blk.vector
            def _(vector):
                for h in range(16):
                    fm, tm = head_ops(h)
                    vector.wait_ge(te.h, te0 + h + 1)
                    vector.tensor_copy(
                        hm["X"].ap()[:, h * 128:h * 128 + 64], tm("u"))
                    vector.tensor_copy(
                        hm["X"].ap()[:, h * 128 + 64:h * 128 + 128],
                        pp[h % 4].ap()[:, 0:64]).then_inc(ev.h, 2)
        for lev in ["A", "A2", "A4", "A8", "A16", "A32"]:
            with p.block() as blk:
                te0, ev0 = te.n, ev.n
                te.n += 16
                ev.n += 16

                @blk.tensor
                def _(tensor, lev=lev):
                    for h in range(16):
                        if h >= 4:
                            tensor.wait_ge(ev.h, ev0 + h - 3)
                        tensor.matmul(pp[h % 4].ap(),
                                      hm[lev].ap()[:, h * 64:(h + 1) * 64],
                                      hm["X"].ap()[:, h * 128:(h + 1) * 128],
                                      start=True, stop=True).then_inc(te.h, 1)

                @blk.vector
                def _(vector, lev=lev):
                    for h in range(16):
                        vector.wait_ge(te.h, te0 + h + 1)
                        vector.tensor_tensor(
                            hm["X"].ap()[:, h * 128:(h + 1) * 128],
                            hm["X"].ap()[:, h * 128:(h + 1) * 128],
                            pp[h % 4].ap(), OP.add).then_inc(ev.h, 1)
        with p.block() as blk:
            te0, ev0 = te.n, ev.n
            te.n += 4 * 16
            ev.n += 4 * 16

            @blk.tensor
            def _(tensor):
                i = 0
                for h in range(16):
                    fm, tm = head_ops(h)
                    X0 = hm["X"].ap()[:, h * 128:h * 128 + 64]
                    X1 = hm["X"].ap()[:, h * 128 + 64:h * 128 + 128]
                    sl = slice(h * 64, (h + 1) * 64)
                    for unit in range(4):
                        if i >= 4:
                            tensor.wait_ge(ev.h, ev0 + i - 3)
                        if unit == 0:
                            tensor.matmul(pp[i % 4].ap()[:, 0:64], X0,
                                          hm["Uw"].ap()[:, sl],
                                          start=True, stop=True
                                          ).then_inc(te.h, 1)
                        elif unit == 1:
                            tensor.matmul(pp[i % 4].ap()[:, 0:64], X0,
                                          tm("w"), start=True, stop=True
                                          ).then_inc(te.h, 1)
                        elif unit == 2:
                            tensor.matmul(pp[i % 4].ap()[:, 0:64], tm("w"),
                                          X1, start=True, stop=False)
                            tensor.matmul(pp[i % 4].ap()[:, 0:64], tm("k"),
                                          tm("v"), start=False, stop=True
                                          ).then_inc(te.h, 1)
                        else:
                            tensor.matmul(pp[i % 4].ap()[:, 0:64], X1,
                                          hm["Uw"].ap()[:, sl],
                                          start=True, stop=False)
                            tensor.matmul(pp[i % 4].ap()[:, 0:64], tm("v"),
                                          hm["Uk"].ap()[:, sl],
                                          start=False, stop=True
                                          ).then_inc(te.h, 1)
                        i += 1

            @blk.vector
            def _(vector):
                i = 0
                for h in range(16):
                    fm, tm = head_ops(h)
                    hb = (h % 2) * 64
                    hp = h // 2
                    sl = slice(h * 64, (h + 1) * 64)
                    for unit in range(4):
                        vector.wait_ge(te.h, te0 + i + 1)
                        if unit == 0:
                            vector.tensor_tensor(hm["G1"].ap()[:, sl],
                                                 pp[i % 4].ap()[:, 0:64],
                                                 fm("rt"), OP.add
                                                 ).then_inc(ev.h, 1)
                        elif unit == 1:
                            vector.tensor_tensor(hm["Mr"].ap()[:, sl],
                                                 pp[i % 4].ap()[:, 0:64],
                                                 id64.ap(), OP.add
                                                 ).then_inc(ev.h, 1)
                        elif unit == 2:
                            vector.tensor_scalar(
                                hm["Cp"].ap()[:, sl], pp[i % 4].ap()[:, 0:64],
                                eL32.ap()[hb:hb + 64, hp:hp + 1],
                                None, OP.mult).then_inc(ev.h, 1)
                        else:
                            vector.tensor_copy(hm["Yl"].ap()[:, sl],
                                               pp[i % 4].ap()[:, 0:64]
                                               ).then_inc(ev.h, 1)
                        i += 1
        with p.block() as blk:
            @blk.sync
            def _(sync):
                for nm, dd in [("Mr", Md), ("Cp", Cd), ("G1", G1d),
                               ("Yl", YLd)]:
                    sync.dma_start(
                        out=dd.ap()[c * 1024:(c + 1) * 1024, :]
                        .rearrange("(h i) j -> i h j", i=64),
                        in_=hm[nm].ap().rearrange("i (h j) -> i h j", j=64)
                    ).then_inc(st.h, 16)
                st.n += 64
        with p.block() as blk:
            @blk.sync
            def _(sync):
                sync.wait_ge(st.h, st.n)
        ELd = p.dram("l3a_ELd", [NCH_L3 * 64, 16], F32)
        with p.block() as blk:
            @blk.sync
            def _(sync):
                for b in range(2):
                    sync.dma_start(
                        out=ELd.ap()[c * 64:(c + 1) * 64, :]
                        .rearrange("i (hp b2) -> i hp b2", b2=2)[:, :, b:b + 1],
                        in_=eL32.ap()[b * 64:(b + 1) * 64, :]
                        .rearrange("i (hp one) -> i hp one", one=1),
                        allow_slow_non_contiguous=True).then_inc(st.h, 16)
                    st.n += 16
        with p.block() as blk:
            @blk.sync
            def _(sync):
                sync.wait_ge(st.h, st.n)


NCH_L3 = 32


def emit_l3_sweep_and_y(p, NCH, Md, Cd, G1d, YLd, Sprevd, Yd, bm_d):
    nc = p.nc
    ld = p.sem("l3s_ld")
    st = p.sem("l3s_st")
    te = p.sem("l3s_te")
    ev = p.sem("l3s_ev")
    with p.scope():
        Ssb = p.sbuf("l3s_S", [64, 16 * 64], BF16)
        mst = p.sbuf("l3s_mst", [64, 16 * 64], BF16)
        cst = p.sbuf("l3s_cst", [64, 16 * 64], BF16)
        elst = p.sbuf("l3s_el", [64, 16], F32)
        bmask = p.sbuf("l3s_bm", [128, 1], F32)
        sin = p.sbuf("l3s_sin", [64, 16 * 64], BF16)
        with p.block() as blk:
            @blk.gpsimd
            def _(gpsimd):
                gpsimd.dma_start(out=bmask.ap(), in_=bm_d.ap()).then_inc(ld.h, 16)
                ld.n += 16
            @blk.vector
            def _(vector):
                vector.wait_ge(ld.h, ld.n)
        cin = p.dram("l3s_cc_in", [1024, 64], BF16)
        for rnd in range(4):
            if rnd == 0:
                with p.block() as blk:
                    @blk.gpsimd
                    def _(gpsimd):
                        gpsimd.memset(Ssb.ap(), 0.0)
            else:
                cout = p.dram(f"l3s_cc_out{rnd}", [4096, 64], BF16,
                              addr_space="Local")
                ccs = p.sem("l3s_cc")
                with p.block() as blk:
                    @blk.sync
                    def _(sync):
                        sync.dma_start(
                            out=cin.ap().rearrange("(h i) j -> i h j", i=64),
                            in_=Ssb.ap().rearrange("i (h j) -> i h j", j=64)
                        ).then_inc(st.h, 16)
                        st.n += 16
                with p.block() as blk:
                    tok = st.n

                    @blk.gpsimd
                    def _(gpsimd, cout=cout):
                        gpsimd.wait_ge(st.h, tok)
                        ccs.inc(gpsimd.collective_compute(
                            "AllGather", OP.bypass,
                            replica_groups=[[0, 1, 2, 3], [4, 5, 6, 7]],
                            ins=[cin.ap().opt()], outs=[cout.ap().opt()]))
                with p.block() as blk:
                    tok = ccs.n

                    @blk.sync
                    def _(sync, cout=cout):
                        sync.wait_ge(ccs.h, tok)
                        prev1024 = p.sync_reg(
                            "prev1024",
                            lambda: p.sync_reg(
                                "prev", lambda: (p.pid4() + 3) % 4) * 1024)
                        sync.dma_start(
                            out=sin.ap().rearrange("i (h j) -> i h j", j=64),
                            in_=cout.ap()[bass.ds(prev1024, 1024), :]
                            .rearrange("(h i) j -> i h j", i=64)
                        ).then_inc(ld.h, 16)
                        ld.n += 16
                    @blk.vector
                    def _(vector):
                        vector.wait_ge(ld.h, ld.n)
                with p.block() as blk:
                    @blk.vector
                    def _(vector):
                        vector.tensor_scalar(Ssb.ap(), sin.ap(),
                                             bmask.ap()[0:64, 0:1], None,
                                             OP.mult)
            last = (rnd == 3)
            for c in range(NCH):
                ld0 = ld.n
                ld.n += 48
                with p.block() as blk:
                    @blk.sync
                    def _(sync, c=c):
                        sync.dma_start(
                            out=mst.ap().rearrange("i (h j) -> i h j", j=64),
                            in_=Md.ap()[c * 1024:(c + 1) * 1024, :]
                            .rearrange("(h i) j -> i h j", i=64)
                        ).then_inc(ld.h, 16)
                        sync.dma_start(
                            out=cst.ap().rearrange("i (h j) -> i h j", j=64),
                            in_=Cd.ap()[c * 1024:(c + 1) * 1024, :]
                            .rearrange("(h i) j -> i h j", i=64)
                        ).then_inc(ld.h, 16)
                        sync.dma_start(
                            out=elst.ap(),
                            in_=p.dram_handles["l3a_ELd"].ap()
                            [c * 64:(c + 1) * 64, :]
                        ).then_inc(ld.h, 16)
                    if last:
                        @blk.sync
                        def _(sync, c=c):
                            sync.wait_ge(ld.h, ld0)
                            sync.dma_start(
                                out=Sprevd.ap()[c * 1024:(c + 1) * 1024, :]
                                .rearrange("(h i) j -> i h j", i=64),
                                in_=Ssb.ap().rearrange("i (h j) -> i h j", j=64)
                            ).then_inc(st.h, 16)
                            st.n += 16
                with p.scope():
                    pps = [p.psum("l3s_pp", [64, 64]) for _ in range(4)]
                    te0, ev0 = te.n, ev.n
                    te.n += 16
                    ev.n += 16
                    with p.block() as blk:
                        @blk.tensor
                        def _(tensor, c=c):
                            tensor.wait_ge(ld.h, ld0 + 48)
                            for h in range(16):
                                if h >= 4:
                                    tensor.wait_ge(ev.h, ev0 + h - 3)
                                tensor.matmul(
                                    pps[h % 4].ap(),
                                    mst.ap()[:, h * 64:(h + 1) * 64],
                                    Ssb.ap()[:, h * 64:(h + 1) * 64],
                                    start=True, stop=True).then_inc(te.h, 1)

                        @blk.vector
                        def _(vector, c=c):
                            for h in range(16):
                                hb = (h % 2) * 64
                                hp = h // 2
                                vector.wait_ge(te.h, te0 + h + 1)
                                vector.scalar_tensor_tensor(
                                    Ssb.ap()[:, h * 64:(h + 1) * 64],
                                    pps[h % 4].ap(),
                                    elst.ap()[:, h:h + 1],
                                    cst.ap()[:, h * 64:(h + 1) * 64],
                                    OP.mult, OP.add).then_inc(ev.h, 1)
        g1st = mst
        ylst = cst
        spst = sin
        yst = p.sbuf("l3s_yst", [128, 8 * 64], BF16)
        for c in range(NCH):
            ld0 = ld.n
            ld.n += 48
            with p.block() as blk:
                @blk.sync
                def _(sync, c=c):
                    for buf, dd in [(g1st, G1d), (ylst, YLd), (spst, Sprevd)]:
                        sync.dma_start(
                            out=buf.ap().rearrange("i (h j) -> i h j", j=64),
                            in_=dd.ap()[c * 1024:(c + 1) * 1024, :]
                            .rearrange("(h i) j -> i h j", i=64)
                        ).then_inc(ld.h, 16)
            with p.scope():
                pps = [p.psum("l3s_ppy", [64, 64]) for _ in range(4)]
                te0, ev0 = te.n, ev.n
                te.n += 16
                ev.n += 16
                with p.block() as blk:
                    @blk.tensor
                    def _(tensor, c=c):
                        tensor.wait_ge(ld.h, ld0 + 48)
                        for h in range(16):
                            if h >= 4:
                                tensor.wait_ge(ev.h, ev0 + h - 3)
                            tensor.matmul(
                                pps[h % 4].ap(),
                                spst.ap()[:, h * 64:(h + 1) * 64],
                                g1st.ap()[:, h * 64:(h + 1) * 64],
                                start=True, stop=True).then_inc(te.h, 1)

                    @blk.vector
                    def _(vector, c=c):
                        for h in range(16):
                            hb = (h % 2) * 64
                            hp = h // 2
                            vector.wait_ge(te.h, te0 + h + 1)
                            vector.tensor_tensor(
                                yst.ap()[hb:hb + 64, hp * 64:(hp + 1) * 64],
                                pps[h % 4].ap(),
                                ylst.ap()[:, h * 64:(h + 1) * 64], OP.add
                                ).then_inc(ev.h, 1)
                with p.block() as blk:
                    @blk.sync
                    def _(sync, c=c):
                        sync.dma_start(
                            out=Yd.ap().rearrange("(hp p) t -> p hp t", p=128)
                            [:, :, c * 64:(c + 1) * 64],
                            in_=yst.ap().rearrange("p (hp t) -> p hp t", t=64)
                        ).then_inc(st.h, 16)
                        st.n += 16
        with p.block() as blk:
            @blk.sync
            def _(sync):
                sync.wait_ge(st.h, st.n)


def emit_l3_epilogue(p, Yd, Rd, KMd, Vdm, SGd, rk_d, lnw_d, lnb_d, g2_d, wo_d):
    nc = p.nc
    EPS = 0.00064
    xres = p.sb["xres"]
    ld = p.sem("l3e_ld")
    te = p.sem("l3e_te")
    fv = p.sem("l3e_fv")
    fs = p.sem("l3e_fs")
    rkpc_d = p.dram("l3a_rkpc", [128, 8], F32, kind="ExternalInput")
    with p.scope():
        s16 = p.sbuf("l3e_s16", [16, TPC], F32)
        rch = p.sbuf("l3e_rch", [128, TPC], BF16)
        kch = p.sbuf("l3e_kch", [128, TPC], BF16)
        prod = p.sbuf("l3e_prod", [128, 512], BF16)
        rkpc = p.sbuf("l3e_rkpc", [128, 8], F32)
        s2 = p.sbuf("l3e_s2", [2, 512], F32)
        ind2 = p.sbuf("l3e_ind2", [128, 2], BF16)
        with p.block() as blk:
            @blk.gpsimd
            def _(gpsimd):
                gpsimd.dma_start(out=rkpc.ap(), in_=rkpc_d.ap()).then_inc(ld.h, 16)
                ld.n += 16
            @blk.vector
            def _(vector):
                vector.wait_ge(ld.h, ld.n)
                vector.tensor_scalar(s2.ap()[0:2, 0:1],
                                     p.sb["iota_p"].ap()[0:2, 0:1], 0.0, None,
                                     OP.mult)
        with p.block() as blk:
            @blk.vector
            def _(vector):
                vector.tensor_scalar(ind2.ap()[:, 1:2],
                                     p.sb["iota_p"].ap()[:, 0:1], 63.5, None,
                                     OP.is_gt)
        with p.block() as blk:
            @blk.vector
            def _(vector):
                vector.tensor_scalar(ind2.ap()[:, 0:1], ind2.ap()[:, 1:2],
                                     -1.0, 1.0, OP.mult, OP.add)
        for dc in range(ND):
            ld0 = ld.n
            ld.n += 32
            with p.block() as blk:
                @blk.sync
                def _(sync, dc=dc):
                    sync.dma_start(out=rch.ap(),
                                   in_=Rd.ap()[dc * 128:(dc + 1) * 128, :]
                                   ).then_inc(ld.h, 16)
                    sync.dma_start(out=kch.ap(),
                                   in_=KMd.ap()[dc * 128:(dc + 1) * 128, :]
                                   ).then_inc(ld.h, 16)
                @blk.vector
                def _(vector):
                    vector.wait_ge(ld.h, ld.n)
            for ts in range(4):
                with p.scope():
                    ps2 = p.psum("l3e_ps2", [2, 512])
                    with p.block() as blk:
                        @blk.vector
                        def _(vector, dc=dc, ts=ts):
                            vector.tensor_tensor(
                                prod.ap(), rch.ap()[:, ts * 512:(ts + 1) * 512],
                                kch.ap()[:, ts * 512:(ts + 1) * 512], OP.mult)
                    with p.block() as blk:
                        @blk.vector
                        def _(vector, dc=dc):
                            vector.tensor_scalar(prod.ap(), prod.ap(),
                                                 rkpc.ap()[:, dc:dc + 1], None,
                                                 OP.mult)
                    with p.block() as blk:
                        @blk.tensor
                        def _(tensor):
                            tensor.matmul(ps2.ap(), ind2.ap(), prod.ap(),
                                          start=True, stop=True)
                    with p.block() as blk:
                        @blk.vector
                        def _(vector):
                            vector.tensor_copy(s2.ap(), ps2.ap())
                    with p.block() as blk:
                        @blk.gpsimd
                        def _(gpsimd, dc=dc, ts=ts):
                            gpsimd.dma_start(
                                out=s16.ap()[2 * dc:2 * dc + 2,
                                             ts * 512:(ts + 1) * 512],
                                in_=s2.ap()).then_inc(ld.h, 16)
                            ld.n += 16
        with p.block() as blk:
            @blk.vector
            def _(vector, tok=ld.n):
                vector.wait_ge(ld.h, tok)
        lnwb = p.sbuf("l3e_lnwb", [128, 1024], F32)
        lnbb = p.sbuf("l3e_lnbb", [128, 1024], F32)
        emit_bcast_row(p, lnw_d, lnwb, "l3e_w")
        emit_bcast_row(p, lnb_d, lnbb, "l3e_b")
        g2sb = p.sbuf("l3e_g2", [128, 1024], BF16)
        sigsb = p.sbuf("l3e_sig", [128, TPC], BF16)
        with p.block() as blk:
            @blk.gpsimd
            def _(gpsimd):
                gpsimd.dma_start(out=g2sb.ap(), in_=g2_d.ap()).then_inc(ld.h, 16)
                ld.n += 16
            @blk.sync
            def _(sync):
                sync.dma_start(out=sigsb.ap(), in_=SGd.ap()).then_inc(ld.h, 16)
                ld.n += 16
            @blk.vector
            def _(vector):
                vector.wait_ge(ld.h, ld.n)
        slab = p.sbuf("l3e_slab", [128, 8 * 128], BF16)
        ytile = p.sbuf("l3e_yt", [128, 1024], BF16)
        vtile = p.sbuf("l3e_vt", [128, 1024], BF16)
        stile = p.sbuf("l3e_st", [128, 16], F32)
        gstat = p.sbuf("l3e_gs", [128, 16], F32)
        gstat2 = p.sbuf("l3e_gs2", [128, 16], F32)
        sqt = p.sbuf("l3e_sqt", [128, 1024], F32)
        tt1 = p.sbuf("l3e_tt1", [128, 1024], F32)
        ygtm = p.sbuf("l3e_ygtm", [128, 1024], BF16)
        ygfm = p.sbuf("l3e_ygfm", [128, 8 * 128], BF16)
        wbo = [p.sbuf("l3e_wbo", [128, 1024], BF16) for _ in range(2)]
        for t in range(NT):
            for src_d, dstb in [(Yd, ytile), (Vdm, vtile)]:
                ld0 = ld.n
                ld.n += 16
                with p.block() as blk:
                    @blk.sync
                    def _(sync, t=t, src_d=src_d):
                        sync.dma_start(
                            out=slab.ap().rearrange("p (c k) -> p c k", c=8),
                            in_=src_d.ap().rearrange("(c p) tt -> p c tt", p=128)
                            [:, :, t * 128:(t + 1) * 128]).then_inc(ld.h, 16)
                    @blk.vector
                    def _(vector):
                        vector.wait_ge(ld.h, ld.n)
                pairs = [(slab.ap()[:, dc * 128:(dc + 1) * 128],
                          dstb.ap()[:, dc * 128:(dc + 1) * 128])
                         for dc in range(ND)]
                emit_transpose_many(p, pairs)
            with p.scope():
                pst = p.psum("l3e_pst", [128, 16], BF16)
                s16b = p.sbuf("l3e_s16b", [16, 128], BF16)
                with p.block() as blk:
                    @blk.vector
                    def _(vector, t=t):
                        vector.tensor_copy(s16b.ap(),
                                           s16.ap()[:, t * 128:(t + 1) * 128])
                with p.block() as blk:
                    @blk.tensor
                    def _(tensor, t=t):
                        tensor.transpose(pst.ap(), s16b.ap(),
                                         p.sb["ident"].ap()[0:16, 0:16])
                with p.block() as blk:
                    @blk.vector
                    def _(vector):
                        vector.tensor_copy(stile.ap(), pst.ap())
            with p.block() as blk:
                @blk.vector
                def _(vector):
                    grp = ytile.ap().rearrange("p (g k) -> p g k", k=64)
                    inst = vector.reduce_sum(gstat.ap(), grp, axis=AX.X)
                    fv.inc(inst, 1)
                @blk.scalar
                def _(scalar):
                    inst = scalar.activation(sqt.ap(), ytile.ap(), AF.Square)
                    fs.inc(inst, 1)
            with p.block() as blk:
                @blk.vector
                def _(vector):
                    grp2 = sqt.ap().rearrange("p (g k) -> p g k", k=64)
                    inst = vector.reduce_sum(gstat2.ap(), grp2, axis=AX.X)
                    fv.inc(inst, 1)
                    vector.wait_ge(fv.h, fv.n)
                    inst = vector.tensor_scalar(gstat.ap(), gstat.ap(),
                                                1.0 / 64.0, None, OP.mult)
                    fv.inc(inst, 1)
                    vector.wait_ge(fv.h, fv.n)
                    inst = vector.tensor_scalar(gstat2.ap(), gstat2.ap(),
                                                1.0 / 64.0, None, OP.mult)
                    fv.inc(inst, 1)
                    vector.wait_ge(fv.h, fv.n)
                    inst = vector.tensor_tensor(sqt.ap()[:, 0:16], gstat.ap(),
                                                gstat.ap(), OP.mult)
                    fv.inc(inst, 1)
                    vector.wait_ge(fv.h, fv.n)
                    inst = vector.tensor_tensor(gstat2.ap(), gstat2.ap(),
                                                sqt.ap()[:, 0:16], OP.subtract)
                    fv.inc(inst, 1)
                    vector.wait_ge(fv.h, fv.n)
                    inst = vector.tensor_scalar(gstat2.ap(), gstat2.ap(), EPS,
                                                None, OP.add)
                    fv.inc(inst, 1)
            with p.block() as blk:
                @blk.scalar
                def _(scalar):
                    inst = scalar.activation(gstat2.ap(), gstat2.ap(), AF.Ln)
                    fs.inc(inst, 1)
                    scalar.wait_ge(fs.h, fs.n)
                    inst = scalar.activation(gstat2.ap(), gstat2.ap(), AF.Exp,
                                             scale=-0.5)
                    fs.inc(inst, 1)
            with p.block() as blk:
                @blk.vector
                def _(vector, t=t):
                    for g in range(16):
                        gs = slice(g * 64, (g + 1) * 64)
                        vector.tensor_scalar(
                            tt1.ap()[:, gs], ytile.ap()[:, gs],
                            gstat.ap()[:, g:g + 1], None, OP.subtract)
                    inst = vector.engine_nop()
                    fv.inc(inst, 1)
                    vector.wait_ge(fv.h, fv.n)
                    for g in range(16):
                        gs = slice(g * 64, (g + 1) * 64)
                        inst = vector.tensor_scalar(
                            tt1.ap()[:, gs], tt1.ap()[:, gs],
                            gstat2.ap()[:, g:g + 1], None, OP.mult)
                    fv.inc(inst, 1)
                    vector.wait_ge(fv.h, fv.n)
                    inst = vector.tensor_tensor(tt1.ap(), tt1.ap(), lnwb.ap(),
                                                OP.mult)
                    fv.inc(inst, 1)
                    vector.wait_ge(fv.h, fv.n)
                    inst = vector.tensor_tensor(tt1.ap(), tt1.ap(), lnbb.ap(),
                                                OP.add)
                    fv.inc(inst, 1)
                    vector.wait_ge(fv.h, fv.n)
                    for g in range(16):
                        gs = slice(g * 64, (g + 1) * 64)
                        inst = vector.scalar_tensor_tensor(
                            tt1.ap()[:, gs], vtile.ap()[:, gs],
                            stile.ap()[:, g:g + 1], tt1.ap()[:, gs],
                            OP.mult, OP.add)
                    fv.inc(inst, 1)
            with p.scope():
                pgm = p.psum("l3e_pgm", [128, 1024])
                po = p.psum("l3e_po", [128, 1024])
                with p.block() as blk:
                    @blk.tensor
                    def _(tensor, t=t):
                        for hf in range(2):
                            tensor.matmul(
                                pgm.ap()[:, hf * 512:(hf + 1) * 512],
                                sigsb.ap()[:, t * 128:(t + 1) * 128],
                                g2sb.ap()[:, hf * 512:(hf + 1) * 512],
                                start=True, stop=True)
                with p.block() as blk:
                    @blk.vector
                    def _(vector, t=t):
                        vector.tensor_tensor(sqt.ap(), tt1.ap(), pgm.ap(),
                                             OP.mult)
                with p.block() as blk:
                    @blk.vector
                    def _(vector):
                        vector.tensor_copy(ygtm.ap(), sqt.ap())
                pairs = [(ygtm.ap()[:, dc * 128:(dc + 1) * 128],
                          ygfm.ap()[:, dc * 128:(dc + 1) * 128])
                         for dc in range(ND)]
                emit_transpose_many(p, pairs)
                ld0 = ld.n
                ld.n += 16 * ND
                te0 = te.n
                te.n += ND
                with p.block() as blk:
                    @blk.gpsimd
                    def _(gpsimd, t=t):
                        for dc in range(ND):
                            if dc >= 2:
                                gpsimd.wait_ge(te.h, te0 + dc - 1)
                            gpsimd.dma_start(
                                out=wbo[dc % 2].ap(),
                                in_=wo_d.ap()[dc * 128:(dc + 1) * 128, :]
                            ).then_inc(ld.h, 16)

                    @blk.tensor
                    def _(tensor, t=t):
                        for dc in range(ND):
                            tensor.wait_ge(ld.h, ld0 + 16 * (dc + 1))
                            for hf in range(2):
                                inst = tensor.matmul(
                                    po.ap()[:, hf * 512:(hf + 1) * 512],
                                    ygfm.ap()[:, dc * 128:(dc + 1) * 128],
                                    wbo[dc % 2].ap()[:, hf * 512:(hf + 1) * 512],
                                    start=(dc == 0), stop=(dc == ND - 1))
                            inst.then_inc(te.h, 1)
                with p.block() as blk:
                    @blk.vector
                    def _(vector, t=t):
                        vector.tensor_tensor(
                            xres.ap()[:, t * 1024:(t + 1) * 1024],
                            xres.ap()[:, t * 1024:(t + 1) * 1024],
                            po.ap(), OP.add)


def emit_knorm(p, sq, eb1, pn2, pnb, ind2, indT, nsc, nsc2, KNd, fc, ts,
               stage, st, ld):
    es = contextlib.ExitStack()
    p.scopes.append(es)
    sqb = p.sbuf("l3_sqb", [128, 512], BF16)
    with p.block() as blk:
        @blk.scalar
        def _(scalar):
            scalar.activation(sq.ap(), eb1.ap()[:, 0:512], AF.Square)
    with p.block() as blk:
        @blk.vector
        def _(vector):
            vector.tensor_copy(sqb.ap(), sq.ap())
    with p.block() as blk:
        @blk.tensor
        def _(tensor):
            tensor.matmul(pn2.ap(), ind2.ap(), sqb.ap(), start=True, stop=True)
    with p.block() as blk:
        @blk.scalar
        def _(scalar):
            scalar.activation(nsc.ap()[0:2, 0:512], pn2.ap(), AF.Sqrt)
    with p.block() as blk:
        @blk.vector
        def _(vector):
            vector.tensor_scalar(nsc.ap()[0:2, 0:512], nsc.ap()[0:2, 0:512],
                                 1e-12, None, OP.max)
    with p.block() as blk:
        @blk.vector
        def _(vector):
            vector.reciprocal(nsc.ap()[0:2, 0:512], nsc.ap()[0:2, 0:512])
    with p.block() as blk:
        @blk.vector
        def _(vector):
            vector.tensor_copy(nsc2.ap()[0:2, 0:512], nsc.ap()[0:2, 0:512])
    with p.block() as blk:
        @blk.tensor
        def _(tensor):
            tensor.matmul(pnb.ap(), indT.ap(), nsc2.ap()[0:2, 0:512],
                          start=True, stop=True)
    with p.block() as blk:
        @blk.vector
        def _(vector):
            vector.tensor_tensor(stage.ap(), eb1.ap()[:, 0:512], pnb.ap(),
                                 OP.mult)
    with p.block() as blk:
        @blk.sync
        def _(sync):
            sync.dma_start(out=KNd.ap()[fc * 128:(fc + 1) * 128,
                                        ts * 512:(ts + 1) * 512],
                           in_=stage.ap()).then_inc(st.h, 16)
            st.n += 16
    with p.block() as blk:
        @blk.vector
        def _(vector, stw=st.n):
            vector.wait_ge(st.h, stw)
    p.scopes.pop()
    es.close()


def emit_L3_attn(p):
    nc = p.nc
    LC = 64
    NCH = TPC // LC
    mix_d = p.dram("l3a_mix", [128, 48], F32, kind="ExternalInput")
    wrkv_d = p.dram("l3a_wrkv", [D, 3 * D], F32, kind="ExternalInput")
    w1_d = p.dram("l3a_w1", [D, 64], F32, kind="ExternalInput")
    w2_d = p.dram("l3a_w2", [64, D], F32, kind="ExternalInput")
    a1_d = p.dram("l3a_a1", [D, 64], F32, kind="ExternalInput")
    a2_d = p.dram("l3a_a2", [64, D], F32, kind="ExternalInput")
    g1_d = p.dram("l3a_g1", [D, 128], F32, kind="ExternalInput")
    g2_d = p.dram("l3a_g2", [128, D], F32, kind="ExternalInput")
    w0_d = p.dram("l3a_w0", [128, 8], F32, kind="ExternalInput")
    a0_d = p.dram("l3a_a0", [128, 8], F32, kind="ExternalInput")
    kk_d = p.dram("l3a_kk", [128, 8], F32, kind="ExternalInput")
    ka_d = p.dram("l3a_ka", [128, 8], F32, kind="ExternalInput")
    rk_d = p.dram("l3a_rk", [1, 1024], F32, kind="ExternalInput")
    lnw_d = p.dram("l3a_lnw", [1, D], F32, kind="ExternalInput")
    lnb_d = p.dram("l3a_lnb", [1, D], F32, kind="ExternalInput")
    wo_d = p.dram("l3a_wo", [D, D], F32, kind="ExternalInput")
    bm_d = p.dram("l3a_bmask", [128, 1], F32, kind="ExternalInput")
    Rd = p.dram("l3a_R", [D, TPC], BF16)
    KMd = p.dram("l3a_KM", [D, TPC], BF16)
    KNd = p.dram("l3a_KN", [D, TPC], BF16)
    Vdm = p.dram("l3a_Vm", [D, TPC], BF16)
    AMd = p.dram("l3a_AM", [D, TPC], BF16)
    EWd = p.dram("l3a_EW", [D, TPC], BF16)
    Yd = p.dram("l3a_Y", [D, TPC], BF16)
    Md = p.dram("l3a_Md", [NCH * 1024, 64], BF16)
    Cd = p.dram("l3a_Cd", [NCH * 1024, 64], BF16)
    G1d = p.dram("l3a_G1d", [NCH * 1024, 64], BF16)
    YLd = p.dram("l3a_YLd", [NCH * 1024, 64], BF16)
    Sprevd = p.dram("l3a_Sp", [NCH * 1024, 64], BF16)
    xres = p.sb["xres"]
    ld = p.sem("l3_ld")
    st = p.sem("l3_st")
    fv = p.sem("l3_fv")
    fs = p.sem("l3_fs")
    fg = p.sem("l3_fg")
    te = p.sem("l3_te")
    ident = p.sb["ident"]

    with p.scope():
        xfm = p.sbuf("l3_xfm", [128, ND * 2049], BF16)
        with p.scope():
            xhat = p.sbuf("l3_xhat", [128, NT * 1024], BF16)
            emit_pre_ln(p, xhat)
            brow = emit_allgather(
                p, "l3_brow", xhat.ap()[127:128, 15 * 1024:16 * 1024],
                [1, 1024], BF16)
            emit_tm_to_fm(p, xhat, xfm, dst_off=1)
        bmask = p.sbuf("l3_bmask", [128, 1], F32)
        mix_sb = p.sbuf("l3_mix", [128, 48], F32)
        coef = {}
        for name, dd in [("w0", w0_d), ("a0", a0_d), ("kk", kk_d),
                         ("ka", ka_d)]:
            coef[name] = p.sbuf("l3_" + name, [128, 8], F32)
        with p.block() as blk:
            @blk.gpsimd
            def _(gpsimd):
                gpsimd.dma_start(out=bmask.ap(), in_=bm_d.ap()).then_inc(ld.h, 16)
                gpsimd.dma_start(out=mix_sb.ap(), in_=mix_d.ap()).then_inc(ld.h, 16)
                for name, dd in [("w0", w0_d), ("a0", a0_d), ("kk", kk_d),
                                 ("ka", ka_d)]:
                    gpsimd.dma_start(out=coef[name].ap(), in_=dd.ap()
                                     ).then_inc(ld.h, 16)
                ld.n += 96
        bcol = p.sbuf("l3_bcol", [128, 8], BF16)
        with p.block() as blk:
            @blk.sync
            def _(sync):
                prev = p.sync_reg("prev", lambda: (p.pid4() + 3) % 4)
                sync.dma_start(
                    out=bcol.ap(),
                    in_=brow.ap()[bass.ds(prev, 1), :]
                    .rearrange("one (c p) -> (one p) c", p=128),
                    allow_slow_non_contiguous=True
                ).then_inc(ld.h, 16)
                ld.n += 16
            @blk.vector
            def _(vector):
                vector.wait_ge(ld.h, ld.n)
        with p.block() as blk:
            @blk.vector
            def _(vector):
                vector.tensor_scalar(bcol.ap(), bcol.ap(), bmask.ap()[:, 0:1],
                                     None, OP.mult)
        with p.block() as blk:
            @blk.vector
            def _(vector):
                for c in range(ND):
                    vector.tensor_copy(xfm.ap()[:, c * 2049:c * 2049 + 1],
                                       bcol.ap()[:, c:c + 1])
        tmp = p.sbuf("l3_tmp", [128, ND * TPC], BF16)
        with p.block() as blk:
            @blk.vector
            def _(vector):
                for c in range(ND):
                    vector.tensor_tensor(
                        tmp.ap()[:, c * TPC:(c + 1) * TPC],
                        xfm.ap()[:, c * 2049:c * 2049 + TPC],
                        xfm.ap()[:, c * 2049 + 1:c * 2049 + TPC + 1],
                        OP.subtract)
        mixbuf = p.sbuf("l3_mixbuf", [128, ND * 512], BF16)
        wsl = [p.sbuf("l3_wsl", [128, 8 * 512], BF16) for _ in range(2)]
        small = p.sbuf("l3_small", [128, 8 * 64], BF16)
        lowfm = p.sbuf("l3_low", [128, 2 * 2048], BF16)
        sig = p.sbuf("l3_sig", [128, TPC], BF16)
        escr = p.sbuf("l3_escr", [128, 512], F32)
        escr2 = p.sbuf("l3_escr2", [128, 512], F32)
        onesf = p.sbuf("l3_onesf", [128, 1], F32)
        nhalf = p.sbuf("l3_nhalf", [128, 1], F32)
        ind2 = p.sbuf("l3_ind2", [128, 2], BF16)
        indT = p.sbuf("l3_indT", [2, 128], BF16)
        with p.block() as blk:
            @blk.gpsimd
            def _(gpsimd):
                gpsimd.memset(onesf.ap(), 1.0)
                gpsimd.memset(nhalf.ap(), -0.5)
            @blk.vector
            def _(vector):
                vector.tensor_scalar(escr.ap()[:, 0:1],
                                     p.sb["iota_p"].ap()[:, 0:1], 64.0, None,
                                     OP.is_ge)
        with p.block() as blk:
            @blk.vector
            def _(vector):
                vector.tensor_scalar(ind2.ap()[:, 1:2], escr.ap()[:, 0:1],
                                     1.0, None, OP.mult)
                vector.tensor_scalar(ind2.ap()[:, 0:1], escr.ap()[:, 0:1],
                                     -1.0, 1.0, OP.mult, OP.add)
        with p.scope():
            ptr8 = p.psum("l3_ptr8", [2, 128], BF16)
            with p.block() as blk:
                @blk.tensor
                def _(tensor):
                    tensor.transpose(ptr8.ap(), ind2.ap(), ident.ap())
            with p.block() as blk:
                @blk.vector
                def _(vector):
                    vector.tensor_copy(indT.ap(), ptr8.ap())

        def make_mix(vector, mi, ts):
            for c in range(ND):
                vector.scalar_tensor_tensor(
                    mixbuf.ap()[:, c * 512:(c + 1) * 512],
                    tmp.ap()[:, c * TPC + ts * 512:c * TPC + (ts + 1) * 512],
                    mix_sb.ap()[:, c * 6 + mi:c * 6 + mi + 1],
                    xfm.ap()[:, c * 2049 + 1 + ts * 512:
                             c * 2049 + 1 + (ts + 1) * 512],
                    OP.mult, OP.add)

        g1sb = p.sbuf("l3_g1sb", [128, 8 * 128], BF16)
        with p.block() as blk:
            @blk.gpsimd
            def _(gpsimd):
                gpsimd.dma_start(
                    out=small.ap().rearrange("p (c k) -> p c k", k=64),
                    in_=w1_d.ap().rearrange("(c p) k -> p c k", p=128)
                ).then_inc(ld.h, 16)
                gpsimd.dma_start(
                    out=g1sb.ap().rearrange("p (c k) -> p c k", c=8),
                    in_=g1_d.ap().rearrange("(c p) k -> p c k", p=128)
                ).then_inc(ld.h, 16)
                ld.n += 32
            @blk.vector
            def _(vector):
                vector.wait_ge(ld.h, ld.n)
        a1sb = p.sbuf("l3_a1sb", [128, 8 * 64], BF16)
        with p.block() as blk:
            @blk.gpsimd
            def _(gpsimd):
                gpsimd.dma_start(
                    out=a1sb.ap().rearrange("p (c k) -> p c k", c=8),
                    in_=a1_d.ap().rearrange("(c p) k -> p c k", p=128)
                ).then_inc(ld.h, 16)
                ld.n += 16
            @blk.vector
            def _(vector):
                vector.wait_ge(ld.h, ld.n)
        for mi, name, Wsb, width, act in [
                (3, "wlow", small, 64, AF.Tanh),
                (4, "alow", a1sb, 64, AF.Identity),
                (5, "siglow", g1sb, 128, AF.Sigmoid)]:
            off = {"wlow": 0, "alow": 2048, "siglow": 4096}[name]
            for ts in range(4):
                with p.scope():
                    pl = p.psum("l3_pl", [128, 512])
                    with p.block() as blk:
                        @blk.vector
                        def _(vector, mi=mi, ts=ts):
                            make_mix(vector, mi, ts)
                    with p.block() as blk:
                        @blk.tensor
                        def _(tensor, Wsb=Wsb, width=width):
                            for dc in range(ND):
                                inst = tensor.matmul(
                                    pl.ap()[0:width, :],
                                    Wsb.ap()[:, dc * width:(dc + 1) * width],
                                    mixbuf.ap()[:, dc * 512:(dc + 1) * 512],
                                    start=(dc == 0), stop=(dc == ND - 1))
                            inst.then_inc(te.h, 1)
                            te.n += 1
                    with p.block() as blk:
                        @blk.scalar
                        def _(scalar, off=off, ts=ts, width=width, act=act,
                              name=name):
                            if name == "siglow":
                                scalar.activation(
                                    sig.ap()[0:width, ts * 512:(ts + 1) * 512],
                                    pl.ap()[0:width, :], act)
                            else:
                                scalar.activation(
                                    lowfm.ap()[0:width, off + ts * 512:
                                               off + (ts + 1) * 512],
                                    pl.ap()[0:width, :], act)
        w2sb = p.sbuf("l3_w2sb", [64, 1024], BF16)
        a2sb = p.sbuf("l3_a2sb", [64, 1024], BF16)
        with p.block() as blk:
            @blk.gpsimd
            def _(gpsimd):
                gpsimd.dma_start(out=w2sb.ap(), in_=w2_d.ap()).then_inc(ld.h, 16)
                gpsimd.dma_start(out=a2sb.ap(), in_=a2_d.ap()).then_inc(ld.h, 16)
                ld.n += 32
            @blk.vector
            def _(vector):
                vector.wait_ge(ld.h, ld.n)
        es2 = contextlib.ExitStack()
        p.scopes.append(es2)
        eb1 = p.sbuf("l3_eb1", [128, 2048], F32)
        eb2 = p.sbuf("l3_eb2", [128, 2048], F32)
        ebo = p.sbuf("l3_ebo", [128, 2048], BF16)
        for which, lowoff, Bsb, outd in [(0, 0, w2sb, EWd), (1, 2048, a2sb, AMd)]:
            for dc in range(ND):
                with p.scope():
                    pl4 = [p.psum("l3_pl2", [128, 512]) for _ in range(4)]
                    with p.block() as blk:
                        @blk.tensor
                        def _(tensor, lowoff=lowoff, Bsb=Bsb, dc=dc):
                            for ts in range(4):
                                tensor.matmul(
                                    pl4[ts].ap(),
                                    Bsb.ap()[:, dc * 128:(dc + 1) * 128],
                                    lowfm.ap()[0:64, lowoff + ts * 512:
                                               lowoff + (ts + 1) * 512],
                                    start=True, stop=True)
                    if which == 1:
                        with p.block() as blk:
                            @blk.scalar
                            def _(scalar, dc=dc):
                                for ts in range(4):
                                    scalar.activation(
                                        ebo.ap()[:, ts * 512:(ts + 1) * 512],
                                        pl4[ts].ap(), AF.Sigmoid,
                                        bias=coef["a0"].ap()[:, dc:dc + 1])
                    else:
                        with p.block() as blk:
                            @blk.vector
                            def _(vector, dc=dc):
                                for ts in range(4):
                                    vector.tensor_scalar(
                                        eb1.ap()[:, ts * 512:(ts + 1) * 512],
                                        pl4[ts].ap(),
                                        coef["w0"].ap()[:, dc:dc + 1], None,
                                        OP.add)
                        with p.block() as blk:
                            @blk.scalar
                            def _(scalar):
                                scalar.activation(eb2.ap(), eb1.ap(), AF.Abs)
                        with p.block() as blk:
                            @blk.scalar
                            def _(scalar):
                                scalar.activation(eb2.ap(), eb2.ap(), AF.Exp,
                                                  scale=-1.0)
                        with p.block() as blk:
                            @blk.scalar
                            def _(scalar):
                                scalar.activation(eb2.ap(), eb2.ap(), AF.Ln,
                                                  bias=onesf.ap()[:, 0:1])
                        with p.block() as blk:
                            @blk.scalar
                            def _(scalar):
                                scalar.activation(eb1.ap(), eb1.ap(), AF.Relu,
                                                  scale=-1.0)
                        with p.block() as blk:
                            @blk.vector
                            def _(vector):
                                vector.tensor_tensor(eb1.ap(), eb1.ap(),
                                                     eb2.ap(), OP.add)
                        with p.block() as blk:
                            @blk.scalar
                            def _(scalar):
                                scalar.activation(ebo.ap(), eb1.ap(), AF.Exp,
                                                  scale=-1.0,
                                                  bias=nhalf.ap()[:, 0:1])
                    with p.block() as blk:
                        @blk.sync
                        def _(sync, outd=outd, dc=dc):
                            sync.dma_start(
                                out=outd.ap()[dc * 128:(dc + 1) * 128, :],
                                in_=ebo.ap()).then_inc(st.h, 16)
                            st.n += 16
                    with p.block() as blk:
                        @blk.vector
                        def _(vector, stw=st.n):
                            vector.wait_ge(st.h, stw)
        p.scopes.pop()
        es2.close()
        kb1 = p.sbuf("l3_kb1", [128, 512], F32)
        kb2 = p.sbuf("l3_kb2", [128, 512], F32)
        kam1 = p.sbuf("l3_kam1", [128, 8], F32)
        with p.block() as blk:
            @blk.vector
            def _(vector):
                vector.tensor_scalar(kam1.ap(), coef["ka"].ap(), -1.0, 1.0,
                                     OP.mult, OP.add)
        amfm = p.sbuf("l3_amfm", [128, 512], BF16)
        rkv_stage = p.sbuf("l3_rkvs", [128, 512], BF16)
        nsc = p.sbuf("l3_nsc", [2, 512], F32)
        nsc2 = p.sbuf("l3_nsc2", [2, 512], BF16)
        sq = p.sbuf("l3_sq", [128, 512], F32)
        for ts in range(4):
            for mi, outd in [(0, Rd), (2, Vdm), (1, KMd)]:
                with p.block() as blk:
                    @blk.vector
                    def _(vector, mi=mi, ts=ts):
                        make_mix(vector, mi, ts)
                with p.scope():
                    prj = [p.psum("l3_prj", [128, 512]) for _ in range(4)]
                    pn2 = p.psum("l3_pn2", [2, 512])
                    pnb = p.psum("l3_pnb", [128, 512])
                    wofs = mi * 1024
                    for half in range(2):
                        ld0 = ld.n
                        ld.n += 16
                        with p.block() as blk:
                            @blk.gpsimd
                            def _(gpsimd, half=half, wofs=wofs):
                                gpsimd.dma_start(
                                    out=wsl[half].ap()
                                    .rearrange("p (c k) -> p c k", c=ND),
                                    in_=wrkv_d.ap()[:, wofs + half * 512:
                                                    wofs + (half + 1) * 512]
                                    .rearrange("(c p) k -> p c k", p=128)
                                ).then_inc(ld.h, 16)
                            @blk.vector
                            def _(vector):
                                vector.wait_ge(ld.h, ld.n)
                        with p.block() as blk:
                            @blk.tensor
                            def _(tensor, half=half):
                                for f in range(4):
                                    for dc in range(ND):
                                        inst = tensor.matmul(
                                            prj[f].ap(),
                                            wsl[half].ap()[:, dc * 512 + f * 128:
                                                           dc * 512 + (f + 1) * 128],
                                            mixbuf.ap()[:, dc * 512:(dc + 1) * 512],
                                            start=(dc == 0), stop=(dc == ND - 1))
                        for f in range(4):
                            fc = half * 4 + f
                            with p.block() as blk:
                                @blk.vector
                                def _(vector, f=f):
                                    vector.tensor_copy(rkv_stage.ap(), prj[f].ap())
                            with p.block() as blk:
                                @blk.sync
                                def _(sync, outd=outd, fc=fc, ts=ts):
                                    sync.dma_start(
                                        out=outd.ap()[fc * 128:(fc + 1) * 128,
                                                      ts * 512:(ts + 1) * 512],
                                        in_=rkv_stage.ap()).then_inc(st.h, 16)
                                    st.n += 16
                            if mi == 1:
                                with p.block() as blk:
                                    @blk.sync
                                    def _(sync, fc=fc, ts=ts):
                                        sync.dma_start(
                                            out=amfm.ap(),
                                            in_=AMd.ap()[fc * 128:(fc + 1) * 128,
                                                         ts * 512:(ts + 1) * 512]
                                        ).then_inc(ld.h, 16)
                                        ld.n += 16
                                    @blk.vector
                                    def _(vector, f=f, fc=fc):
                                        vector.wait_ge(ld.h, ld.n)
                                        vector.tensor_scalar(
                                            kb1.ap()[:, 0:512], prj[f].ap(),
                                            coef["kk"].ap()[:, fc:fc + 1], None,
                                            OP.mult)
                                with p.block() as blk:
                                    @blk.scalar
                                    def _(scalar):
                                        scalar.activation(sq.ap(),
                                                          eb1.ap()[:, 0:512],
                                                          AF.Square)
                                with p.block() as blk:
                                    @blk.vector
                                    def _(vector):
                                        vector.tensor_copy(
                                            eb2.ap()[:, 0:512].bitcast(F32)
                                            if False else eb2.ap()[:, 0:512],
                                            sq.ap())
                                with p.block() as blk:
                                    @blk.tensor
                                    def _(tensor):
                                        tensor.matmul(pn2.ap(), ind2.ap(),
                                                      ebo2_bf(p, eb2, sq),
                                                      start=True, stop=True)                                             if False else None
                                emit_knorm(p, sq, kb1, pn2, pnb, ind2, indT,
                                           nsc, nsc2, KNd, fc, ts, rkv_stage,
                                           st, ld)
                                with p.block() as blk:
                                    @blk.vector
                                    def _(vector, f=f, fc=fc):
                                        vector.tensor_scalar(
                                            kb2.ap()[:, 0:512], amfm.ap(),
                                            coef["ka"].ap()[:, fc:fc + 1],
                                            kam1.ap()[:, fc:fc + 1],
                                            OP.mult, OP.add)
                                with p.block() as blk:
                                    @blk.vector
                                    def _(vector, f=f):
                                        vector.tensor_tensor(
                                            rkv_stage.ap(), prj[f].ap(),
                                            kb2.ap()[:, 0:512], OP.mult)
                                with p.block() as blk:
                                    @blk.sync
                                    def _(sync, fc=fc, ts=ts):
                                        sync.dma_start(
                                            out=KMd.ap()[fc * 128:(fc + 1) * 128,
                                                         ts * 512:(ts + 1) * 512],
                                            in_=rkv_stage.ap()
                                        ).then_inc(st.h, 16)
                                        st.n += 16
                                with p.block() as blk:
                                    @blk.vector
                                    def _(vector, stw=st.n):
                                        vector.wait_ge(st.h, stw)
        SGd = p.dram("l3a_SG", [128, TPC], BF16)
        with p.block() as blk:
            @blk.sync
            def _(sync):
                sync.dma_start(out=SGd.ap(), in_=sig.ap()).then_inc(st.h, 16)
                st.n += 16
        with p.block() as blk:
            @blk.sync
            def _(sync):
                sync.wait_ge(st.h, st.n)

    LN2C = LC
    with p.scope():
        tri_i = p.sbuf("l3_tri_i", [64, 64], BF16)
        tri_s = p.sbuf("l3_tri_s", [64, 64], BF16)
        mko = p.sbuf("l3_mko", [64, 64], BF16)
        mki = p.sbuf("l3_mki", [64, 64], BF16)
        id64 = p.sbuf("l3_id64", [64, 64], BF16)
        with p.block() as blk:
            @blk.vector
            def _(vector):
                vector.tensor_tensor(tri_i.ap(), p.sb["iota_p"].ap()[0:64, 0:64],
                                     p.sb["iota_f"].ap()[0:64, 0:64], OP.is_le)
                vector.tensor_tensor(tri_s.ap(), p.sb["iota_p"].ap()[0:64, 0:64],
                                     p.sb["iota_f"].ap()[0:64, 0:64], OP.is_lt)
                vector.tensor_copy(mko.ap(), tri_s.ap())
                vector.tensor_copy(mki.ap(), tri_i.ap())
                vector.tensor_tensor(id64.ap(), p.sb["iota_p"].ap()[0:64, 0:64],
                                     p.sb["iota_f"].ap()[0:64, 0:64], OP.is_equal)
        mko16 = p.sbuf("l3_mko16", [64, 16 * 64], BF16)
        mki16 = p.sbuf("l3_mki16", [64, 16 * 64], BF16)
        mklo16 = p.sbuf("l3_mklo16", [64, 16 * 64], BF16)
        p.sb["l3_mko16"] = mko16
        p.sb["l3_mki16"] = mki16
        p.sb["l3_mklo16"] = mklo16
        with p.block() as blk:
            @blk.vector
            def _(vector):
                for h in range(16):
                    sl = slice(h * 64, (h + 1) * 64)
                    vector.tensor_copy(mko16.ap()[:, sl], mko.ap())
                    vector.tensor_copy(mki16.ap()[:, sl], mki.ap())
                    vector.tensor_tensor(
                        mklo16.ap()[:, sl], p.sb["iota_p"].ap()[0:64, 0:64],
                        p.sb["iota_f"].ap()[0:64, 0:64], OP.is_gt)
        slabs = {n: p.sbuf("l3_sl_" + n, [128, 8 * 64], BF16)
                 for n in ["r", "km", "kn", "v", "am", "ew"]}
        efm = {n: p.sbuf("l3_e_" + n, [128, 8 * 64], BF16)
               for n in ["exc", "inc", "inv"]}
        smat = {n: p.sbuf("l3_m_" + n, [128, 8 * 64], BF16)
                for n in ["ut", "wt", "kt", "rt"]}
        tmm = {n: p.sbuf("l3_t_" + n, [64, 8 * 128], BF16)
               for n in ["u", "w", "k", "v"]}
        ewT = p.sbuf("l3_ewT", [64, 8 * 128], BF16)
        eL32 = p.sbuf("l3_eL32", [128, 8], F32)
        cumsb = p.sbuf("l3_cumsb", [64, 2 * 8 * 128], BF16)
        hm = {n: p.sbuf("l3_h_" + n, [64, 16 * 64], BF16)
              for n in ["A", "AT", "B", "Uw", "Uk", "A2", "A2T", "A4", "A4T",
                        "A8", "A8T", "A16", "A16T", "A32", "G1", "Yl",
                        "Mr", "Cp"]}
        hm["X"] = p.sbuf("l3_h_X", [64, 16 * 128], BF16)
        fvv = p.sem("l3_fv4")
        for c in range(NCH):
            ld0 = ld.n
            with p.block() as blk:
                @blk.sync
                def _(sync, c=c):
                    for n, dd in [("r", Rd), ("km", KMd), ("kn", KNd),
                                  ("v", Vdm), ("am", AMd), ("ew", EWd)]:
                        sync.dma_start(
                            out=slabs[n].ap().rearrange("p (c2 k) -> p c2 k", c2=8),
                            in_=dd.ap().rearrange("(c2 p) t -> p c2 t", p=128)
                            [:, :, c * LC:(c + 1) * LC]).then_inc(ld.h, 16)
                    ld.n += 96
                @blk.vector
                def _(vector):
                    vector.wait_ge(ld.h, ld.n)
            pairs = [(slabs["ew"].ap()[:, hp * 64:(hp + 1) * 64],
                      ewT.ap()[:, hp * 128:(hp + 1) * 128])
                     for hp in range(8)]
            emit_transpose_many(p, pairs)
            with p.scope():
                pcum = [p.psum("l3_pcum", [64, 512]) for _ in range(2)]
                for grp in range(2):
                    with p.block() as blk:
                        @blk.tensor
                        def _(tensor, grp=grp):
                            tensor.matmul(pcum[0].ap(), tri_i.ap(),
                                          ewT.ap()[:, grp * 512:(grp + 1) * 512],
                                          start=True, stop=True)
                            tensor.matmul(pcum[1].ap(), tri_s.ap(),
                                          ewT.ap()[:, grp * 512:(grp + 1) * 512],
                                          start=True, stop=True)
                    with p.block() as blk:
                        @blk.vector
                        def _(vector, grp=grp):
                            vector.tensor_scalar(
                                cumsb.ap()[:, grp * 512:(grp + 1) * 512],
                                pcum[0].ap(), -1.0, None, OP.mult)
                            vector.tensor_scalar(
                                cumsb.ap()[:, 1024 + grp * 512:
                                           1024 + (grp + 1) * 512],
                                pcum[1].ap(), -1.0, None, OP.mult)
            pairs = [(cumsb.ap()[:, hp * 128:(hp + 1) * 128],
                      efm["inc"].ap()[:, hp * 64:(hp + 1) * 64])
                     for hp in range(8)] + \
                    [(cumsb.ap()[:, 1024 + hp * 128:1024 + (hp + 1) * 128],
                      efm["exc"].ap()[:, hp * 64:(hp + 1) * 64])
                     for hp in range(8)]
            emit_transpose_many(p, pairs)
            with p.block() as blk:
                @blk.scalar
                def _(scalar):
                    scalar.activation(efm["inv"].ap(), efm["inc"].ap(), AF.Exp,
                                      scale=-1.0).then_inc(fs.h, 1)
                    fs.n += 1
                    scalar.wait_ge(fs.h, fs.n)
                    scalar.activation(efm["inc"].ap(), efm["inc"].ap(), AF.Exp)
                    scalar.activation(efm["exc"].ap(), efm["exc"].ap(), AF.Exp
                                      ).then_inc(fs.h, 1)
                    fs.n += 1
            with p.block() as blk:
                @blk.vector
                def _(vector):
                    vector.wait_ge(fs.h, fs.n)
                    vector.tensor_copy(
                        eL32.ap(),
                        efm["inc"].ap().rearrange("p (hp k) -> p hp k", k=64)
                        [:, :, 63:64].squeeze(2))
                    for hp in range(8):
                        sl = slice(hp * 64, (hp + 1) * 64)
                        vector.scalar_tensor_tensor(
                            smat["ut"].ap()[:, sl], slabs["kn"].ap()[:, sl],
                            -1.0, efm["exc"].ap()[:, sl], OP.mult, OP.mult)
                        vector.tensor_tensor(
                            smat["kt"].ap()[:, sl], slabs["km"].ap()[:, sl],
                            efm["inv"].ap()[:, sl], OP.mult)
                        vector.tensor_tensor(
                            smat["rt"].ap()[:, sl], slabs["r"].ap()[:, sl],
                            efm["inc"].ap()[:, sl], OP.mult)
                    inst = vector.engine_nop()
                    inst.then_inc(fv.h, 1)
                    fv.n += 1
                @blk.gpsimd
                def _(gpsimd):
                    gpsimd.wait_ge(fs.h, fs.n)
                    for hp in range(8):
                        sl = slice(hp * 64, (hp + 1) * 64)
                        gpsimd.tensor_tensor(
                            smat["wt"].ap()[:, sl], slabs["kn"].ap()[:, sl],
                            slabs["am"].ap()[:, sl], OP.mult)
                    inst = gpsimd.engine_nop()
                    inst.then_inc(fg.h, 1)
                    fg.n += 1
            with p.block() as blk:
                @blk.vector
                def _(vector):
                    for hp in range(8):
                        sl = slice(hp * 64, (hp + 1) * 64)
                        vector.tensor_tensor(
                            smat["wt"].ap()[:, sl], smat["wt"].ap()[:, sl],
                            efm["inv"].ap()[:, sl], OP.mult)
            pairs = []
            for n_src, n_dst in [("ut", "u"), ("wt", "w"), ("kt", "k")]:
                for hp in range(8):
                    pairs.append((smat[n_src].ap()[:, hp * 64:(hp + 1) * 64],
                                  tmm[n_dst].ap()[:, hp * 128:(hp + 1) * 128]))
            for hp in range(8):
                pairs.append((slabs["v"].ap()[:, hp * 64:(hp + 1) * 64],
                              tmm["v"].ap()[:, hp * 128:(hp + 1) * 128]))
            emit_transpose_many(p, pairs)
            emit_l3_head_algebra(p, c, smat, tmm, efm, hm, mko, mki, id64,
                                 Md, Cd, G1d, YLd, st, eL32=eL32)
    emit_l3_sweep_and_y(p, NCH, Md, Cd, G1d, YLd, Sprevd, Yd, bm_d)
    emit_l3_epilogue(p, Yd, Rd, KMd, Vdm, SGd, rk_d, lnw_d, lnb_d, g2_d, wo_d)



BLOCK_EMITTERS = {"L3_ffn": emit_L3_ffn, "L2_ffn": emit_L2_ffn, "L1_ffn": emit_L1_ffn, "L0_ffn": emit_L0_ffn, "L1_attn": emit_L1_attn, "L2_attn": emit_L2_attn, "L0_attn": emit_L0_attn, "L3_attn": emit_L3_attn}
BLOCK_PREPS = {"L3_ffn": prep_L3_ffn, "L2_ffn": prep_L2_ffn, "L1_ffn": prep_L1_ffn, "L0_ffn": prep_L0_ffn, "L1_attn": prep_L1_attn, "L2_attn": prep_L2_attn, "L0_attn": prep_L0_attn, "L3_attn": prep_L3_attn}



def build_program(core, blocks):
    nc = bass.Bass("TRN2", target_bir_lowering=False)
    with contextlib.ExitStack() as stack:
        p = Prog(nc, core, stack)
        xin = p.dram("xin", [TPC, D], F32, kind="ExternalInput")
        xout = p.dram("xout", [TPC, D], F32, kind="ExternalOutput")
        xres = p.sbuf("xres", [128, NT * 1024], F32, persist=True)
        ld0 = p.sem("ld0")
        with p.block() as blk:
            @blk.sync
            def _(sync):
                ld0.inc(sync.dma_start(
                    out=xres.ap().rearrange("p (t d) -> p t d", d=1024),
                    in_=xin.ap().rearrange("(t p) d -> p t d", p=128)), 16)

            @blk.vector
            def _(vector):
                vector.wait_ge(ld0.h, ld0.n)
        emit_const_setup(p)
        for b in blocks:
            with p.scope():
                BLOCK_EMITTERS[b](p)
        st0 = p.sem("st0")
        with p.block() as blk:
            @blk.sync
            def _(sync):
                st0.inc(sync.dma_start(
                    out=xout.ap().rearrange("(t p) d -> p t d", p=128),
                    in_=xres.ap().rearrange("p (t d) -> p t d", d=1024)), 16)
                sync.wait_ge(st0.h, st0.n)
    return nc


def run_blocks(blocks, x_full, inputs):
    xr = x_full.reshape(NCORES, TPC, D)
    in_maps = []
    nc = build_program(0, blocks)
    for c in range(NCORES):
        m = {"xin": np.ascontiguousarray(xr[c])}
        for b in blocks:
            m.update(BLOCK_PREPS[b](c, inputs))
        in_maps.append(m)
    res = run_bass_kernel_spmd(nc, in_maps, core_ids=list(range(NCORES)))
    out = np.concatenate([res.results[c]["xout"] for c in range(NCORES)], 0)
    return out.reshape(B, S, D)


def kernel(**inputs):
    x = np.ascontiguousarray(inputs["L0_attn_inputs"], dtype=np.float32)
    return run_blocks(ALL_BLOCKS, x, inputs)
